# version 03
# baseline (speedup 1.0000x reference)
import math
from contextlib import ExitStack

import numpy as np
import ml_dtypes

import concourse.bass as bass
import concourse.mybir as mybir
from concourse.alu_op_type import AluOpType
from concourse.bass_utils import run_bass_kernel_spmd

F32 = mybir.dt.float32
BF16 = mybir.dt.bfloat16
AF = mybir.ActivationFunctionType
AX = mybir.AxisListType.X
BFD = ml_dtypes.bfloat16

NEG = -30000.0
NH, HD, DM = 16, 128, 2048
HIDN = 5632
NHC = HIDN // 128
TOK, OWN, NT = 2048, 1024, 16
C0, C1 = 1792, 1664
NG = 2048
W_E = 1920
NWEL = HIDN * DM
GCH = 2048
NWCH = NWEL // (128 * GCH)


def _bucket_delta(d):
    n = np.asarray(np.maximum(d, 0), np.int64)
    nf = np.maximum(n, 1).astype(np.float32)
    large = (np.log(nf / np.float32(16.0)) / np.float32(math.log(8.0))
             * np.float32(16.0)).astype(np.int32) + 16
    large = np.minimum(large, 31)
    return np.where(n < 16, n, large).astype(np.int64)


def _host_constants(half):
    c = {}
    n = np.arange(NG)
    for name, dd in (("oh_e", C0 + 127 - n), ("oh_o", C1 + 127 + 256 * half - n)):
        oh = np.zeros((33, NG), np.float32)
        valid = (dd >= 0) & (dd <= 2047)
        bk = _bucket_delta(np.maximum(dd, 0))
        oh[bk[valid], n[valid]] = 1.0
        oh[32, ~valid] = 1.0
        c[name] = oh
    inv = (1.0 / np.power(np.float32(10000.0),
                          np.arange(0, HD, 2, dtype=np.float32) / np.float32(HD))
           ).astype(np.float32)
    kap = np.arange(NT)
    m = np.where(kap % 2 == 0, kap + half, kap - half)
    pos_loc = (m[:, None] * 128 + np.arange(128)[None, :]).reshape(-1).astype(np.float32)
    pos_own = ((2 * np.arange(8)[:, None] + half) * 128
               + np.arange(128)[None, :]).reshape(-1).astype(np.float32)

    def tabs(pos, scale):
        ang = pos[None, :] * inv[:, None]
        cos = (np.cos(ang) * scale).astype(np.float32)
        sin = (np.sin(ang) * scale).astype(np.float32)
        return (np.concatenate([cos, cos], 0).astype(BFD),
                np.concatenate([-sin, sin], 0).astype(BFD))

    c["cq"], c["sq"] = tabs(pos_own, np.float32(1.0 / math.sqrt(HD)))
    c["ck"], c["sk"] = tabs(pos_loc, np.float32(1.0))
    c["ident"] = np.eye(128, dtype=BFD)
    jm = np.zeros((128, 128), np.float32)
    jm[np.arange(128), 127 - np.arange(128)] = 1.0
    c["jmat"] = jm.astype(BFD)
    pw = np.zeros((128, 128), np.float32)
    pw[(np.arange(128) + 64) % 128, np.arange(128)] = 1.0
    c["pswap"] = pw.astype(BFD)
    c["ones_f"] = np.ones((128, 128), np.float32)
    return c


class Sem:
    def __init__(self, ctx, nc, name):
        self.h = ctx.enter_context(nc.semaphore(name))
        self.n = 0

    def inc(self, instr, k=1):
        instr.then_inc(self.h, k)
        self.n += k
        return instr

    def wait(self, eng, val=None):
        v = self.n if val is None else val
        if v > 0:
            eng.wait_ge(self.h, v)


SERIAL = 0

def build_nc(taps=(), stop_after="full"):
    taps = set(taps)
    nc = bass.Bass("TRN2", target_bir_lowering=False)

    def din(name, shape, dt=F32):
        return nc.dram_tensor(name, shape, dt, kind="ExternalInput")

    x_loc = din("x_loc", [TOK, DM])
    x_own = din("x_own", [OWN, DM])
    qkv_w = din("qkv_w", [DM, 3 * DM])
    dense_w = din("dense_w", [DM, DM])
    w1 = din("w1", [HIDN, DM])
    w2 = din("w2", [HIDN, DM])
    w3 = din("w3", [DM, HIDN])
    rel_t = din("rel_t", [32, NH])
    oh_e = din("oh_e", [33, NG])
    oh_o = din("oh_o", [33, NG])
    cq_t = din("cq", [128, OWN], BF16)
    sq_t = din("sq", [128, OWN], BF16)
    ck_t = din("ck", [128, TOK], BF16)
    sk_t = din("sk", [128, TOK], BF16)
    ident_t = din("ident", [128, 128], BF16)
    jmat_t = din("jmat", [128, 128], BF16)
    pswap_t = din("pswap", [128, 128], BF16)
    ones_t = din("ones_f", [128, 128])

    out_t = nc.dram_tensor("out", [OWN, DM], F32, kind="ExternalOutput")
    tap_shapes = {
        "xn": ([DM, TOK], BF16), "qs": ([NH * 128, OWN], BF16),
        "kd": ([NH * 128, TOK], BF16), "vd": ([NH * TOK, HD], BF16),
        "gd": ([2 * NH, NG], F32), "ctx": ([NH * 128, OWN], BF16),
        "sd": ([DM, OWN], BF16), "thr": ([128, 4], F32),
        "hid": ([128, NHC * OWN], BF16), "sg": ([DM, OWN], BF16),
        "p00": ([128, 4352], BF16), "s00": ([128, 64], F32),
    }
    tap_t = {tp: nc.dram_tensor("tap_" + tp, *tap_shapes[tp], kind="ExternalOutput")
             for tp in taps if tp in tap_shapes}

    g_e_d = nc.dram_tensor("g_e_d", [NH, NG], BF16)
    g_o_d = nc.dram_tensor("g_o_d", [NH, NG], BF16)
    k_d = nc.dram_tensor("k_d", [NH * 128, TOK], BF16)
    v_d = nc.dram_tensor("v_d", [NH * TOK, HD], BF16)
    xn_d = nc.dram_tensor("xn_d", [DM, OWN], BF16)
    s_d = nc.dram_tensor("s_d", [DM, OWN], BF16)
    sg_d = nc.dram_tensor("sg_d", [DM, OWN], BF16)

    ctx = ExitStack()
    E = ctx.enter_context

    ident = E(nc.sbuf_tensor("c_ident", [128, 128], BF16))
    jmat = E(nc.sbuf_tensor("c_jmat", [128, 128], BF16))
    pswap = E(nc.sbuf_tensor("c_pswap", [128, 128], BF16))
    ones = E(nc.sbuf_tensor("c_ones", [128, 128], F32))
    thr_sb = E(nc.sbuf_tensor("thr_sb", [128, 4], F32))
    epsb = E(nc.sbuf_tensor("c_eps", [128, 1], F32))

    names = ["s_c", "s_x", "s_g", "s_ln", "s_sq", "s_tp", "s_xe", "s_xnd",
             "s_w", "s_mq", "s_eq", "s_rq", "s_kv", "s_gm", "s_ga",
             "s_h", "s_sc", "s_mx", "s_ex", "s_dm", "s_pt", "s_pe", "s_pv",
             "s_cx", "s_d3", "s_de", "s_sd", "s_f", "s_qz", "s_wt", "s_we",
             "s_m4", "s_sg", "s_hv", "s_fs", "s_sgd", "s_fx", "s_ft", "s_fo",
             "s_out", "s_tap", "s_vch", "s_sw", "s_ach"]
    S = {nm: Sem(ctx, nc, nm) for nm in names}
    (s_c, s_x, s_g, s_ln, s_sq, s_tp, s_xe, s_xnd, s_w, s_mq, s_eq, s_rq,
     s_kv, s_gm, s_ga, s_h, s_sc, s_mx, s_ex, s_dm, s_pt, s_pe, s_pv, s_cx,
     s_d3, s_de, s_sd, s_f, s_qz, s_wt, s_we, s_m4, s_sg, s_hv, s_fs, s_sgd,
     s_fx, s_ft, s_fo, s_out, s_tap, s_vch, s_sw, s_ach) = (S[nm] for nm in names)

    blk = E(nc.Block())

    open_stacks = []

    def finish():
        @blk.sync
        def _(s):
            s_out.wait(s)
            s_tap.wait(s)
        for st in reversed(open_stacks):
            st.close()
        ctx.close()
        return nc

    phA = ExitStack()
    open_stacks.append(phA)
    xn_fm = phA.enter_context(nc.sbuf_tensor("xn_fm", [128, NT * DM], BF16))
    QS = phA.enter_context(nc.sbuf_tensor("QS", [128, NH * OWN], BF16))

    phR = ExitStack()
    open_stacks.append(phR)
    cq_sb = phR.enter_context(nc.sbuf_tensor("cq_sb", [128, OWN], BF16))
    sq_sb = phR.enter_context(nc.sbuf_tensor("sq_sb", [128, OWN], BF16))
    ck_sb = phR.enter_context(nc.sbuf_tensor("ck_sb", [128, TOK], BF16))
    sk_sb = phR.enter_context(nc.sbuf_tensor("sk_sb", [128, TOK], BF16))

    ph0 = ExitStack()
    open_stacks.append(ph0)
    tab_aug = ph0.enter_context(nc.sbuf_tensor("tab_aug", [33, NH], F32))
    oh_sb = ph0.enter_context(nc.sbuf_tensor("oh_sb", [33, 2 * NG], F32))
    g_sb = ph0.enter_context(nc.sbuf_tensor("g_sb", [NH, 2 * NG], F32))
    xbuf = ph0.enter_context(nc.sbuf_tensor("xbuf", [128, 2 * DM], F32))
    sqj = ph0.enter_context(nc.sbuf_tensor("sqj", [128, DM], BF16))
    xnt = ph0.enter_context(nc.sbuf_tensor("xnt", [128, 2 * DM], BF16))
    st0 = ph0.enter_context(nc.sbuf_tensor("st0", [128, 64], F32))
    g_ps = ph0.enter_context(nc.psum_tensor("g_ps", [128, NG], F32))
    tp_ps = ph0.enter_context(nc.psum_tensor("tp_ps", [128, 2048], BF16))

    @blk.sync
    def _(s):
        for dst, src in ((ident, ident_t), (jmat, jmat_t), (pswap, pswap_t),
                         (ones, ones_t),
                         (cq_sb, cq_t), (sq_sb, sq_t), (ck_sb, ck_t), (sk_sb, sk_t)):
            s_c.inc(s.dma_start(dst[:, :], src[:, :]), 16)
        s_c.inc(s.dma_start(tab_aug[0:32, :], rel_t[:, :]), 16)
        s_c.inc(s.dma_start(oh_sb[:, 0:NG], oh_e[:, :]), 16)
        s_c.inc(s.dma_start(oh_sb[:, NG:], oh_o[:, :]), 16)

    @blk.scalar
    def _(sc):
        sc.preload_act_table(AF.Sqrt)

    @blk.vector
    def _(v):
        s_g.inc(v.memset(epsb[:, :], 1e-5), 1)
        s_g.inc(v.memset(tab_aug[32:33, :], NEG), 1)

    for rnd in range(2):
        @blk.tensor
        def _(t, rnd=rnd):
            s_c.wait(t)
            s_g.wait(t)
            for j in range(4):
                mm = t.matmul(g_ps[0:NH, 512 * j:512 * (j + 1)], tab_aug[:, :],
                              oh_sb[:, rnd * NG + 512 * j:rnd * NG + 512 * (j + 1)],
                              start=True, stop=True)
            t.drain()
            s_g.inc(t.nop(), 1)

        @blk.vector
        def _(v, rnd=rnd):
            s_g.wait(v)
            cp = v.tensor_copy(g_sb[:, rnd * NG:(rnd + 1) * NG], g_ps[0:NH, :])
            s_g.inc(cp, 1)

    @blk.gpsimd
    def _(g):
        s_g.wait(g)
        s_g.inc(g.dma_start(g_e_d[:, :], g_sb[:, 0:NG]), 16)
        s_g.inc(g.dma_start(g_o_d[:, :], g_sb[:, NG:]), 16)
        if "gd" in tap_t:
            s_g.wait(g)
            s_tap.inc(g.dma_start(tap_t["gd"][0:NH, :], g_sb[:, 0:NG]), 16)
            s_tap.inc(g.dma_start(tap_t["gd"][NH:, :], g_sb[:, NG:]), 16)

    ln_done = []
    for t_i in range(NT):
        par = t_i % 2
        xb = xbuf[:, par * DM:(par + 1) * DM]
        xo = xnt[:, par * DM:(par + 1) * DM]
        sb = par * 32

        @blk.gpsimd
        def _(s, t_i=t_i, xb=xb):
            if t_i >= 2:
                s.wait_ge(s_ln.h, ln_done[t_i - 2])
            s_x.inc(s.dma_start(xb, x_loc[t_i * 128:(t_i + 1) * 128, :]), 16)

        @blk.scalar
        def _(sc, t_i=t_i, xb=xb, sb=sb):
            s_x.wait(sc)
            if t_i >= 2:
                sc.wait_ge(s_ln.h, ln_done[t_i - 2])
            s_ach.inc(sc.activation(sqj[:, :], xb, AF.Square,
                                    accum_out=st0[:, sb + 16:sb + 17]), 1)
            s_ach.wait(sc)
            s_sq.inc(sc.nop(), 1)

        @blk.vector
        def _(v, xb=xb, sb=sb):
            s_x.wait(v)
            s_vch.inc(v.reduce_sum(st0[:, sb:sb + 1], xb, axis=AX), 1)
            s_vch.wait(v)
            s_sq.wait(v)
            s_vch.inc(v.tensor_scalar(st0[:, sb + 4:sb + 5], st0[:, sb:sb + 1],
                                      1.0 / DM, None, AluOpType.mult), 1)
            s_vch.wait(v)
            s_vch.inc(v.tensor_tensor(st0[:, sb + 8:sb + 9], st0[:, sb + 4:sb + 5],
                                      st0[:, sb + 4:sb + 5], AluOpType.mult), 1)
            s_vch.wait(v)
            st = v.scalar_tensor_tensor(st0[:, sb + 12:sb + 13], st0[:, sb + 16:sb + 17],
                                        1.0 / DM, st0[:, sb + 8:sb + 9],
                                        AluOpType.mult, AluOpType.subtract)
            s_vch.inc(st, 1)
            s_vch.wait(v)
            s_ln.inc(v.nop(), 1)

        @blk.scalar
        def _(sc, sb=sb):
            s_ln.wait(sc)
            sc.wait_ge(s_g.h, 1)
            s_ach.inc(sc.activation(st0[:, sb + 20:sb + 21], st0[:, sb + 12:sb + 13],
                                    AF.Sqrt, bias=epsb[:, 0:1]), 1)
            s_ach.wait(sc)
            s_sq.inc(sc.nop(), 1)

        @blk.vector
        def _(v, xb=xb, xo=xo, sb=sb):
            s_sq.wait(v)
            s_vch.inc(v.reciprocal(st0[:, sb + 24:sb + 25], st0[:, sb + 20:sb + 21]), 1)
            s_vch.wait(v)
            cp = v.tensor_scalar(xo, xb, st0[:, sb + 4:sb + 5], st0[:, sb + 24:sb + 25],
                                 AluOpType.subtract, AluOpType.mult)
            s_ln.inc(cp, 1)
        ln_done.append(s_ln.n)

        for grp in range(2):
            gl = 2 * t_i + grp

            @blk.tensor
            def _(t, grp=grp, gl=gl, xo=xo):
                s_ln.wait(t)
                if gl >= 2:
                    t.wait_ge(s_xe.h, gl - 1)
                for fc in range(8 * grp, 8 * grp + 8):
                    mm = t.transpose(
                        tp_ps[:, (gl % 2) * 1024 + (fc % 8) * 128:
                              (gl % 2) * 1024 + (fc % 8 + 1) * 128],
                        xo[:, fc * 128:(fc + 1) * 128], ident[:, :])
                t.drain()
                s_tp.inc(t.nop(), 1)

            @blk.vector
            def _(v, t_i=t_i, grp=grp, gl=gl):
                s_tp.wait(v)
                dst = bass.AP(xn_fm, (8 * grp) * DM + t_i * 128,
                              [[NT * DM, 128], [DM, 8], [1, 128]])
                cp = v.tensor_copy(dst, tp_ps[:, (gl % 2) * 1024:(gl % 2 + 1) * 1024])
                s_xe.inc(cp, 1)

    @blk.scalar
    def _(sc):
        sc.preload_act_table(AF.Exp)

    @blk.sync
    def _(s):
        s_xe.wait(s)
        for k in range(16):
            src = bass.AP(xn_fm, k * DM, [[NT * DM, 128], [256, 8], [1, 128]])
            s_xnd.inc(s.dma_start(xn_d[k * 128:(k + 1) * 128, :], src), 16)
        if "xn" in tap_t:
            for k in range(16):
                s_tap.inc(s.dma_start(tap_t["xn"][k * 128:(k + 1) * 128, :],
                                      xn_fm[:, k * DM:(k + 1) * DM]), 16)

    ph0.close()
    open_stacks.remove(ph0)

    if stop_after == "ph0":
        return finish()

    ph1 = ExitStack()
    open_stacks.append(ph1)
    wstr = ph1.enter_context(nc.sbuf_tensor("wstr", [128, 2 * 8192], BF16))
    eh = ph1.enter_context(nc.sbuf_tensor("eh", [128, 2 * 512], BF16))
    rof = ph1.enter_context(nc.sbuf_tensor("rof", [128, 2 * 512], BF16))
    ub = ph1.enter_context(nc.sbuf_tensor("ub", [128, 2 * 512], BF16))
    ko_sb = ph1.enter_context(nc.sbuf_tensor("ko_sb", [128, 4 * 512], BF16))
    vo_sb = ph1.enter_context(nc.sbuf_tensor("vo_sb", [128, 4 * 512], BF16))
    qk_ps = ph1.enter_context(nc.psum_tensor("qk_ps", [128, 2048], F32))
    sw_ps = ph1.enter_context(nc.psum_tensor("sw_ps", [128, 1024], F32))

    NSTR = 12
    mq_done, eq_hist, rq_hist, kv_hist = [], [], [], []
    rope_pend = []

    for si in range(NSTR):
        @blk.gpsimd
        def _(g, si=si):
            if si >= 2:
                g.wait_ge(s_mq.h, mq_done[si - 2])
            else:
                g.wait_ge(s_xe.h, 32)
                s_g.wait(g)
            src = bass.AP(qkv_w, si * 512,
                          [[3 * DM, 128], [128 * 3 * DM, 16], [1, 512]])
            s_w.inc(g.dma_start(wstr[:, (si % 2) * 8192:(si % 2 + 1) * 8192], src), 16)

        wb = (si % 2) * 8192
        if si < 8:
            groups = [(oc4, tg) for oc4 in range(4) for tg in range(2 if si < 4 else 4)]
        else:
            groups = [(tb,) for tb in range(16)]

        def emit_swap(qg):
            @blk.tensor
            def _(t, qg=qg):
                t.wait_ge(s_eq.h, eq_hist[qg])
                if qg >= 2:
                    t.wait_ge(s_rq.h, qg - 1)
                mm = t.matmul(sw_ps[:, (qg % 2) * 512:(qg % 2 + 1) * 512],
                              pswap[:, :], eh[:, (qg % 2) * 512:(qg % 2 + 1) * 512],
                              start=True, stop=True)
                t.drain()
                s_sw.inc(t.nop(), 1)

        def emit_rope(qg, qsi, qdesc):
            oc4, tg = qdesc

            @blk.vector
            def _(v, qg=qg, qsi=qsi, oc4=oc4, tg=tg):
                s_sw.wait(v, qg + 1)
                ee = eh[:, (qg % 2) * 512:(qg % 2 + 1) * 512]
                ro = rof[:, (qg % 2) * 512:(qg % 2 + 1) * 512]
                uu = ub[:, (qg % 2) * 512:(qg % 2 + 1) * 512]
                Ct, St = (cq_sb, sq_sb) if qsi < 4 else (ck_sb, sk_sb)
                co = tg * 512
                v.tensor_tensor(ro, ee, Ct[:, co:co + 512], AluOpType.mult)
                v.tensor_tensor(uu, sw_ps[:, (qg % 2) * 512:(qg % 2 + 1) * 512],
                                St[:, co:co + 512], AluOpType.mult)
                if qsi < 4:
                    h = qsi * 4 + oc4
                    dst = QS[:, h * OWN + co: h * OWN + co + 512]
                else:
                    kg = qg - 32
                    if kg >= 4:
                        v.wait_ge(s_kv.h, kv_hist[kg - 4])
                    dst = ko_sb[:, (kg % 4) * 512:(kg % 4 + 1) * 512]
                fin = v.tensor_tensor(dst, ro, uu, AluOpType.add)
                s_rq.inc(fin, 1)
            rq_hist.append(s_rq.n)

            if qsi >= 4:
                @blk.sync
                def _(s, qg=qg, qsi=qsi, oc4=oc4, tg=tg):
                    h = (qsi - 4) * 4 + oc4
                    s_rq.wait(s)
                    kg = qg - 32
                    dma = s.dma_start(
                        bass.AP(k_d, (h * 128) * TOK + tg * 512,
                                [[TOK, 128], [1, 512]]),
                        ko_sb[:, (kg % 4) * 512:(kg % 4 + 1) * 512])
                    s_kv.inc(dma, 16)
                kv_hist.append(s_kv.n)

        for gdesc in groups:
            gidx = len(eq_hist)
            bank = gidx % 4
            if si == 8 and gdesc == (0,):
                while rope_pend:
                    qg, qsi, qdesc = rope_pend.pop(0)
                    emit_swap(qg)
                    emit_rope(qg, qsi, qdesc)

            @blk.tensor
            def _(t, si=si, gdesc=gdesc, gidx=gidx, bank=bank, wb=wb):
                if gidx == 0:
                    s_xe.wait(t)
                t.wait_ge(s_w.h, 16 * (si + 1))
                if gidx >= 4:
                    t.wait_ge(s_eq.h, gidx - 3)
                for k in range(16):
                    if si < 4:
                        oc4, tg = gdesc
                        rhs = bass.AP(xn_fm, k * DM + tg * 1024,
                                      [[NT * DM, 128], [256, 4], [1, 128]])
                        lhsT = wstr[:, wb + k * 512 + oc4 * 128:
                                    wb + k * 512 + oc4 * 128 + 128]
                    elif si < 8:
                        oc4, tg = gdesc
                        rhs = xn_fm[:, k * DM + tg * 512: k * DM + (tg + 1) * 512]
                        lhsT = wstr[:, wb + k * 512 + oc4 * 128:
                                    wb + k * 512 + oc4 * 128 + 128]
                    else:
                        (tb,) = gdesc
                        lhsT = xn_fm[:, k * DM + tb * 128: k * DM + (tb + 1) * 128]
                        rhs = wstr[:, wb + k * 512: wb + (k + 1) * 512]
                    mm = t.matmul(qk_ps[:, bank * 512:(bank + 1) * 512], lhsT, rhs,
                                  start=(k == 0), stop=(k == 15))
                t.drain()
                s_mq.inc(t.nop(), 1)

            if si < 8:
                @blk.scalar
                def _(sc, gidx=gidx, bank=bank):
                    s_mq.wait(sc)
                    if gidx >= 2:
                        sc.wait_ge(s_rq.h, rq_hist[gidx - 2] if gidx - 2 < len(rq_hist)
                                   else gidx - 1)
                    cp = sc.copy(eh[:, (gidx % 2) * 512:(gidx % 2 + 1) * 512],
                                 qk_ps[:, bank * 512:(bank + 1) * 512])
                    s_eq.inc(cp, 1)
                eq_hist.append(s_eq.n)
                rope_pend.append((gidx, si, gdesc))
                if len(rope_pend) > 1:
                    qg, qsi, qdesc = rope_pend.pop(0)
                    emit_swap(qg)
                    emit_rope(qg, qsi, qdesc)
            else:
                (tb,) = gdesc
                vg = gidx - 96

                @blk.scalar
                def _(sc, bank=bank, vg=vg):
                    s_mq.wait(sc)
                    if vg >= 4:
                        sc.wait_ge(s_kv.h, kv_hist[64 + vg - 4])
                    cp = sc.copy(vo_sb[:, (vg % 4) * 512:(vg % 4 + 1) * 512],
                                 qk_ps[:, bank * 512:(bank + 1) * 512])
                    s_eq.inc(cp, 1)
                eq_hist.append(s_eq.n)
                rq_hist.append(s_rq.n)

                @blk.sync
                def _(s, si=si, tb=tb, vg=vg):
                    s_eq.wait(s)
                    dst = bass.AP(v_d, ((si - 8) * 4 * TOK + tb * 128) * HD,
                                  [[HD, 128], [TOK * HD, 4], [1, 128]])
                    dma = s.dma_start(dst, vo_sb[:, (vg % 4) * 512:(vg % 4 + 1) * 512])
                    s_kv.inc(dma, 16)
                kv_hist.append(s_kv.n)
        mq_done.append(s_mq.n)

    @blk.sync
    def _(s):
        if "qs" in tap_t:
            s_rq.wait(s)
            for h in range(NH):
                s_tap.inc(s.dma_start(tap_t["qs"][h * 128:(h + 1) * 128, :],
                                      QS[:, h * OWN:(h + 1) * OWN]), 16)
        if "kd" in tap_t:
            s_kv.wait(s)
            s_tap.inc(s.dma_start(tap_t["kd"][:, :], k_d[:, :]), 16)
        if "vd" in tap_t:
            s_kv.wait(s)
            s_tap.inc(s.dma_start(tap_t["vd"][:, :], v_d[:, :]), 16)

    ph1.close()
    open_stacks.remove(ph1)
    phR.close()
    open_stacks.remove(phR)

    if stop_after == "ph1":
        return finish()

    CTX = phA.enter_context(nc.sbuf_tensor("CTX", [128, NH * OWN], BF16))
    ph2 = ExitStack()
    open_stacks.append(ph2)
    khb = ph2.enter_context(nc.sbuf_tensor("khb", [128, 2 * TOK], BF16))
    vhb = ph2.enter_context(nc.sbuf_tensor("vhb", [128, 2 * TOK], BF16))
    ree = ph2.enter_context(nc.sbuf_tensor("ree", [128, 2 * W_E], BF16))
    reo = ph2.enter_context(nc.sbuf_tensor("reo", [128, 2 * W_E], BF16))
    pbf = ph2.enter_context(nc.sbuf_tensor("pbf", [128, 3072], BF16))
    pts = ph2.enter_context(nc.sbuf_tensor("pts", [128, 2 * 512], BF16))
    dmat = ph2.enter_context(nc.sbuf_tensor("dmat", [128, 2 * 128], BF16))
    sst = ph2.enter_context(nc.sbuf_tensor("sst", [128, 64], F32))
    gbuf = ph2.enter_context(nc.sbuf_tensor("gbuf", [128, 2 * GCH], F32))
    gjunk = ph2.enter_context(nc.sbuf_tensor("gjunk", [128, GCH], BF16))
    gpart = ph2.enter_context(nc.sbuf_tensor("gpart", [128, 3 * 192], F32))
    gtot = ph2.enter_context(nc.sbuf_tensor("gtot", [128, 16], F32))
    gth1 = ph2.enter_context(nc.sbuf_tensor("gth1", [1, 4], F32))
    sca = ph2.enter_context(nc.psum_tensor("sca", [128, 1024], F32))
    scb = ph2.enter_context(nc.psum_tensor("scb", [128, 2048], F32))
    cxp = ph2.enter_context(nc.psum_tensor("cxp", [128, 1024], F32))

    wlist = [w1, w2, w3]
    NG_ALL = 3 * NWCH
    gm_hist, ga_hist = [], []

    def emit_gamma_chunk(gi):
        wi, j = divmod(gi, NWCH)

        @blk.gpsimd
        def _(g, gi=gi, wi=wi, j=j):
            if gi >= 2:
                g.wait_ge(s_ga.h, ga_hist[gi - 2])
            else:
                s_kv.wait(g)
            src = bass.AP(wlist[wi], j * 128 * GCH, [[GCH, 128], [1, GCH]])
            s_gm.inc(g.dma_start(gbuf[:, (gi % 2) * GCH:(gi % 2 + 1) * GCH], src), 16)
        gm_hist.append(s_gm.n)

        if gi % 2 == 1:
            @blk.vector
            def _(v, gi=gi, wi=wi, j=j):
                s_gm.wait(v, gm_hist[gi])
                r = v.reduce_sum(gpart[:, wi * 192 + 4 * j: wi * 192 + 4 * j + 1],
                                 gbuf[:, (gi % 2) * GCH:(gi % 2 + 1) * GCH],
                                 axis=AX, apply_absolute_value=True)
                s_ga.inc(r, 1)
        else:
            @blk.scalar
            def _(sc, gi=gi, wi=wi, j=j):
                s_gm.wait(sc, gm_hist[gi])
                r = sc.activation(gjunk[:, :],
                                  gbuf[:, (gi % 2) * GCH:(gi % 2 + 1) * GCH],
                                  AF.Abs, accum_out=gpart[:, wi * 192 + 4 * j: wi * 192 + 4 * j + 1])
                s_ach.inc(r, 1)
                s_ach.wait(sc)
                s_ga.inc(sc.nop(), 1)
        ga_hist.append(s_ga.n)

    def klen_of(qt):
        return 256 * (qt + 1)

    pairs = [(0, 4), (1, 5), (2, 6), (3, 7)]
    cx_hist = []
    sc_pair0 = []
    gamma_iter = iter(range(NG_ALL))
    pe_after_qt = {}
    pt_after_qt = {}
    n_ptgroups = 0

    for h in range(NH):
        par = h % 2

        qengine = blk.sync if SERIAL == -8 else blk.scalar

        @qengine
        def _(s, h=h, par=par):
            if h == 0:
                s_kv.wait(s)
                s_g.wait(s)
            if h >= 2:
                s.wait_ge(s_cx.h, cx_hist[h - 2])
            if SERIAL != -9 and h >= 1:
                s.wait_ge(s_cx.h, cx_hist[h - 1])
            if SERIAL == 4 and h >= 1:
                s.wait_ge(s_cx.h, cx_hist[h - 1])
            if SERIAL == 5 and h >= 1:
                s.wait_ge(s_cx.h, cx_hist[h - 1])
            s_h.inc(s.dma_start(khb[:, par * TOK:(par + 1) * TOK],
                                bass.AP(k_d, h * 128 * TOK, [[TOK, 128], [1, TOK]])), 16)
            if SERIAL == 7 and h >= 1:
                s.wait_ge(s_cx.h, cx_hist[h - 1])
            s_h.inc(s.dma_start(
                vhb[:, par * TOK:(par + 1) * TOK],
                bass.AP(v_d, h * TOK * HD, [[HD, 128], [128 * HD, 16], [1, 128]])), 16)
            if SERIAL == 6 and h >= 1:
                s.wait_ge(s_cx.h, cx_hist[h - 1])
            s_h.inc(s.dma_start(ree[:, par * W_E:(par + 1) * W_E],
                                bass.AP(g_e_d, h * NG, [[1, 128], [1, W_E]])), 16)
            s_h.inc(s.dma_start(reo[:, par * W_E:(par + 1) * W_E],
                                bass.AP(g_o_d, h * NG, [[1, 128], [1, W_E]])), 16)

        if SERIAL != 3:
            for _i in range(9):
                gi = next(gamma_iter, None)
                if gi is not None:
                    emit_gamma_chunk(gi)

        for pi, (qa, qb) in enumerate(pairs):
            if pi == 1:
                sc_pair0.append(s_sc.n)
            dm_snap = {}
            for stream, qt in ((0, qa), (1, qb)):
                ps = sca if stream == 0 else scb
                kl = klen_of(qt)
                nk = kl // 128

                @blk.tensor
                def _(t, h=h, par=par, pi=pi, stream=stream, qt=qt, ps=ps,
                      kl=kl, nk=nk):
                    if pi == 0:
                        t.wait_ge(s_h.h, 64 * (h + 1))
                        if h == 0:
                            t.wait_ge(s_rq.h, 32)
                    if stream in pe_after_qt:
                        t.wait_ge(s_pe.h, pe_after_qt[stream])
                    if SERIAL >= 1:
                        s_pv.wait(t)
                        s_pe.wait(t)
                        s_cx.wait(t)
                    kh = khb[:, par * TOK:(par + 1) * TOK]
                    for c in range((kl + 511) // 512):
                        w = min(512, kl - c * 512)
                        mm = t.matmul(ps[:, c * 512: c * 512 + w],
                                      QS[:, h * OWN + qt * 128: h * OWN + (qt + 1) * 128],
                                      kh[:, c * 512: c * 512 + w], start=True, stop=True)
                    for kap in range(nk):
                        tt = 2 * qt - kap
                        s0 = (C0 if kap % 2 == 0 else C1) - 128 * tt
                        rsb = ree if kap % 2 == 0 else reo
                        mm = t.matmul(ps[:, kap * 128:(kap + 1) * 128], jmat[:, :],
                                      rsb[:, par * W_E + s0: par * W_E + s0 + 128],
                                      start=False, stop=True, skip_group_check=True)
                    t.drain()
                    s_sc.inc(t.nop(), 1)

                @blk.vector
                def _(v, stream=stream, ps=ps, kl=kl):
                    s_sc.wait(v)
                    if stream in pt_after_qt:
                        v.wait_ge(s_pt.h, pt_after_qt[stream])
                    r = v.reduce_max(sst[:, stream * 32:stream * 32 + 1], ps[:, 0:kl],
                                     axis=AX, negate=True)
                    s_vch.inc(r, 1)
                    s_vch.wait(v)
                    s_mx.inc(v.nop(), 1)

                @blk.scalar
                def _(sc, stream=stream, ps=ps, kl=kl):
                    s_mx.wait(sc)
                    po = 0 if stream == 0 else 1024
                    a = sc.activation(pbf[:, po:po + kl], ps[:, 0:kl], AF.Exp,
                                      bias=sst[:, stream * 32:stream * 32 + 1],
                                      accum_out=sst[:, stream * 32 + 4:stream * 32 + 5])
                    s_ach.inc(a, 1)
                    s_ach.wait(sc)
                    s_ex.inc(sc.nop(), 1)

                @blk.vector
                def _(v, stream=stream):
                    s_ex.wait(v)
                    s_vch.inc(v.reciprocal(sst[:, stream * 32 + 8:stream * 32 + 9],
                                           sst[:, stream * 32 + 4:stream * 32 + 5]), 1)
                    s_vch.wait(v)
                    d = v.tensor_scalar(dmat[:, stream * 128:(stream + 1) * 128],
                                        ident[:, :], sst[:, stream * 32 + 8:stream * 32 + 9],
                                        None, AluOpType.mult)
                    s_dm.inc(d, 1)
                dm_snap[stream] = s_dm.n

            if h == 0 and pi == 1 and ("p00" in tap_t):
                @blk.gpsimd
                def _(g):
                    g.wait_ge(s_pv.h, 4)
                    s_tap.inc(g.dma_start(tap_t["p00"][:, 0:3072], pbf[:, :]), 16)
                    s_tap.inc(g.dma_start(tap_t["p00"][:, 3072:3328], dmat[:, :]), 16)
                    s_tap.inc(g.dma_start(tap_t["p00"][:, 3328:4352], pts[:, :]), 16)
                    s_tap.inc(g.dma_start(tap_t["s00"][:, :], sst[:, :]), 16)

                @blk.tensor
                def _(t):
                    s_tap.wait(t)

            for stream, qt in ((0, qa), (1, qb)):
                ps = sca if stream == 0 else scb
                kl = klen_of(qt)
                nk = kl // 128
                nbank = 2 if stream == 0 else 4
                ngrp = (nk + 3) // 4
                poff = 0 if stream == 0 else 1024
                for g in range(ngrp):
                    klo, khi = 4 * g, min(4 * g + 4, nk)
                    gpar = n_ptgroups % 2
                    my_pt_index = n_ptgroups

                    @blk.tensor
                    def _(t, stream=stream, ps=ps, g=g, klo=klo, khi=khi,
                          nbank=nbank, poff=poff, dmv=dm_snap[stream]):
                        if g == 0:
                            t.wait_ge(s_dm.h, dmv)
                        if SERIAL >= 2:
                            s_dm.wait(t)
                            s_ex.wait(t)
                        bank = g % nbank
                        for kk in range(klo, khi):
                            mm = t.matmul(
                                ps[:, bank * 512 + (kk - klo) * 128:
                                   bank * 512 + (kk - klo + 1) * 128],
                                pbf[:, poff + kk * 128: poff + (kk + 1) * 128],
                                dmat[:, stream * 128:(stream + 1) * 128],
                                start=True, stop=True)
                        t.drain()
                        s_pt.inc(t.nop(), 1)

                    @blk.vector
                    def _(v, ps=ps, g=g, klo=klo, khi=khi, nbank=nbank,
                          gpar=gpar, my=my_pt_index):
                        s_pt.wait(v)
                        if my >= 2:
                            v.wait_ge(s_pv.h, my - 1)
                        bank = g % nbank
                        cp = v.tensor_copy(
                            pts[:, gpar * 512: gpar * 512 + (khi - klo) * 128],
                            ps[:, bank * 512: bank * 512 + (khi - klo) * 128])
                        s_pe.inc(cp, 1)

                    @blk.tensor
                    def _(t, h=h, par=par, stream=stream, qt=qt, g=g, klo=klo,
                          khi=khi, nk=nk, gpar=gpar):
                        s_pe.wait(t)
                        if g == 0 and h >= 1:
                            t.wait_ge(s_cx.h, cx_hist[h - 1])
                        vh = vhb[:, par * TOK:(par + 1) * TOK]
                        for kk in range(klo, khi):
                            mm = t.matmul(
                                cxp[:, stream * 512 + (qt % 4) * 128:
                                    stream * 512 + (qt % 4 + 1) * 128],
                                vh[:, kk * 128:(kk + 1) * 128],
                                pts[:, gpar * 512 + (kk - klo) * 128:
                                    gpar * 512 + (kk - klo + 1) * 128],
                                start=(kk == 0), stop=(kk == nk - 1),
                                skip_group_check=True)
                        t.drain()
                        s_pv.inc(t.nop(), 1)
                    n_ptgroups += 1
                pe_after_qt[stream] = s_pe.n
                pt_after_qt[stream] = s_pt.n

        for stream in (0, 1):
            @blk.vector
            def _(v, h=h, stream=stream):
                s_pv.wait(v)
                cp = v.tensor_copy(CTX[:, h * OWN + stream * 512:
                                       h * OWN + (stream + 1) * 512],
                                   cxp[:, stream * 512:(stream + 1) * 512])
                s_cx.inc(cp, 1)
        cx_hist.append(s_cx.n)

    for gi in gamma_iter:
        emit_gamma_chunk(gi)

    @blk.vector
    def _(v):
        s_ga.wait(v)
        for wi in range(3):
            r = v.reduce_sum(gtot[:, 4 * wi:4 * wi + 1],
                             bass.AP(gpart, wi * 192, [[3 * 192, 128], [4, NWCH]]),
                             axis=AX)
            s_vch.inc(r, 1)
            s_vch.wait(v)
        s_ga.inc(v.nop(), 1)

    @blk.tensor
    def _(t):
        s_ga.wait(t)
        s_pe.wait(t)
        for wi in range(3):
            mm = t.matmul(sca[0:1, wi:wi + 1], gtot[:, 4 * wi:4 * wi + 1], ones[:, 0:1],
                          start=True, stop=True)
        t.drain()
        s_ga.inc(t.nop(), 1)

    @blk.scalar
    def _(sc):
        s_ga.wait(sc)
        cp = sc.activation(gth1[0:1, 0:3], sca[0:1, 0:3], AF.Copy,
                           scale=0.5 / float(NWEL), bias=0.5e-8)
        s_ach.inc(cp, 1)
        s_ach.wait(sc)
        s_ga.inc(sc.nop(), 1)

    @blk.tensor
    def _(t):
        s_ga.wait(t)
        for wi in range(3):
            mm = t.matmul(sca[:, 512 + wi:513 + wi], ones[0:1, :], gth1[0:1, wi:wi + 1],
                          start=True, stop=True)
        t.drain()
        s_ga.inc(t.nop(), 1)

    @blk.vector
    def _(v):
        s_ga.wait(v)
        cp = v.tensor_copy(thr_sb[:, 0:3], sca[:, 512:515])
        s_ga.inc(cp, 1)

    @blk.sync
    def _(s):
        if "thr" in tap_t:
            s_ga.wait(s)
            s_tap.inc(s.dma_start(tap_t["thr"][:, :], thr_sb[:, :]), 16)
        if "ctx" in tap_t:
            s_cx.wait(s)
            for h in range(NH):
                s_tap.inc(s.dma_start(tap_t["ctx"][h * 128:(h + 1) * 128, :],
                                      CTX[:, h * OWN:(h + 1) * OWN]), 16)

    ph2.close()
    open_stacks.remove(ph2)

    if stop_after == "ph2":
        return finish()

    ph3 = ExitStack()
    open_stacks.append(ph3)
    dstr = ph3.enter_context(nc.sbuf_tensor("dstr", [128, 2 * 8192], BF16))
    dso = ph3.enter_context(nc.sbuf_tensor("dso", [128, 4 * 512], BF16))
    d_ps = ph3.enter_context(nc.psum_tensor("d_ps", [128, 2048], F32))
    d3_hist, de_hist, sd_hist = [], [], []

    for si in range(4):
        @blk.gpsimd
        def _(g, si=si):
            if si >= 2:
                g.wait_ge(s_d3.h, d3_hist[si - 2])
            else:
                g.wait_ge(s_cx.h, 32)
                s_ga.wait(g)
            src = bass.AP(dense_w, si * 512, [[DM, 128], [128 * DM, 16], [1, 512]])
            s_w.inc(g.dma_start(dstr[:, (si % 2) * 8192:(si % 2 + 1) * 8192], src), 16)

        for oc4 in range(4):
            for tg in range(2):
                gl = si * 8 + oc4 * 2 + tg
                bank = gl % 4

                @blk.tensor
                def _(t, si=si, oc4=oc4, tg=tg, gl=gl, bank=bank):
                    if gl == 0:
                        s_cx.wait(t)
                        s_ga.wait(t)
                    t.wait_ge(s_w.h, s_w.n)
                    if gl >= 4:
                        t.wait_ge(s_de.h, de_hist[gl - 4])
                    wb = (si % 2) * 8192
                    for k in range(16):
                        mm = t.matmul(d_ps[:, bank * 512:(bank + 1) * 512],
                                      dstr[:, wb + k * 512 + oc4 * 128:
                                           wb + k * 512 + oc4 * 128 + 128],
                                      CTX[:, k * OWN + tg * 512: k * OWN + (tg + 1) * 512],
                                      start=(k == 0), stop=(k == 15))
                    t.drain()
                    s_d3.inc(t.nop(), 1)

                @blk.scalar
                def _(sc, gl=gl, bank=bank):
                    s_d3.wait(sc)
                    if gl >= 4:
                        sc.wait_ge(s_sd.h, sd_hist[gl - 4])
                    cp = sc.copy(dso[:, (gl % 4) * 512:(gl % 4 + 1) * 512],
                                 d_ps[:, bank * 512:(bank + 1) * 512])
                    s_de.inc(cp, 1)
                de_hist.append(s_de.n)

                @blk.sync
                def _(s, si=si, oc4=oc4, tg=tg, gl=gl):
                    s_de.wait(s)
                    oc = si * 4 + oc4
                    dma = s.dma_start(
                        bass.AP(s_d, (oc * 128) * OWN + tg * 512, [[OWN, 128], [1, 512]]),
                        dso[:, (gl % 4) * 512:(gl % 4 + 1) * 512])
                    s_sd.inc(dma, 16)
                sd_hist.append(s_sd.n)
        d3_hist.append(s_d3.n)

    @blk.sync
    def _(s):
        if "sd" in tap_t:
            s_sd.wait(s)
            s_tap.inc(s.dma_start(tap_t["sd"][:, :], s_d[:, :]), 16)

    ph3.close()
    open_stacks.remove(ph3)

    if stop_after == "ph3":
        return finish()

    phA.close()
    open_stacks.remove(phA)

    phB = ExitStack()
    open_stacks.append(phB)
    HIDB = phB.enter_context(nc.sbuf_tensor("HIDB", [128, NHC * OWN], BF16))
    ph4a = ExitStack()
    open_stacks.append(ph4a)
    xn_o = ph4a.enter_context(nc.sbuf_tensor("xn_o", [128, 16 * OWN], BF16))
    wfb = ph4a.enter_context(nc.sbuf_tensor("wfb", [128, 4 * DM], F32))
    qza = ph4a.enter_context(nc.sbuf_tensor("qza", [128, DM], BF16))
    qzb = ph4a.enter_context(nc.sbuf_tensor("qzb", [128, DM], BF16))
    qzw = ph4a.enter_context(nc.sbuf_tensor("qzw", [128, 2 * DM], BF16))
    wT = ph4a.enter_context(nc.sbuf_tensor("wT", [128, 4 * DM], BF16))
    sgh = ph4a.enter_context(nc.sbuf_tensor("sgh", [128, 512], BF16))
    tq_ps = ph4a.enter_context(nc.psum_tensor("tq_ps", [128, 4096], BF16))
    x_ps = ph4a.enter_context(nc.psum_tensor("x_ps", [128, 2048], F32))

    @blk.scalar
    def _(sc):
        sc.preload_act_table(AF.Sigmoid)

    @blk.gpsimd
    def _(s):
        s_xnd.wait(s)
        s_sd.wait(s)
        src = bass.AP(xn_d, 0, [[OWN, 128], [128 * OWN, 16], [1, OWN]])
        s_f.inc(s.dma_start(xn_o[:, :], src), 16)

    f_hist, qz_hist, wt_hist, we_hist, m4_hist, hv_hist = [], [], [], [], [], []

    for hc in range(NHC):
        par = hc % 2
        for wi, wsrc in ((0, w1), (1, w2)):
            li = 2 * hc + wi

            @blk.gpsimd
            def _(s, hc=hc, wi=wi, wsrc=wsrc, li=li, par=par):
                if li >= 4:
                    s.wait_ge(s_qz.h, qz_hist[li - 4])
                dma = s.dma_start(wfb[:, (2 * par + wi) * DM:(2 * par + wi + 1) * DM],
                                  wsrc[hc * 128:(hc + 1) * 128, :])
                s_f.inc(dma, 16)
            f_hist.append(s_f.n)

            @blk.vector
            def _(v, li=li, wi=wi, par=par):
                s_f.wait(v, f_hist[li])
                s_ga.wait(v)
                if li >= 2:
                    v.wait_ge(s_wt.h, wt_hist[li - 2])
                wf = wfb[:, (2 * par + wi) * DM:(2 * par + wi + 1) * DM]
                v.tensor_scalar(qza[:, :], wf, thr_sb[:, wi:wi + 1], None,
                                AluOpType.is_ge)
                v.tensor_scalar(qzb[:, :], wf, -1.0, thr_sb[:, wi:wi + 1],
                                AluOpType.mult, AluOpType.is_ge)
                q = v.tensor_tensor(qzw[:, wi * DM:(wi + 1) * DM], qza[:, :], qzb[:, :],
                                    AluOpType.subtract)
                s_qz.inc(q, 1)
            qz_hist.append(s_qz.n)

            @blk.tensor
            def _(t, li=li, wi=wi):
                s_qz.wait(t)
                if li < 2:
                    t.wait_ge(s_de.h, 32)
                if li >= 2:
                    t.wait_ge(s_we.h, we_hist[li - 2])
                for k in range(16):
                    mm = t.transpose(tq_ps[:, wi * 2048 + k * 128:wi * 2048 + (k + 1) * 128],
                                     qzw[:, wi * DM + k * 128: wi * DM + (k + 1) * 128],
                                     ident[:, :])
                t.drain()
                s_wt.inc(t.nop(), 1)
            wt_hist.append(s_wt.n)

            @blk.scalar
            def _(sc, li=li, wi=wi, par=par):
                s_wt.wait(sc)
                if li >= 4:
                    sc.wait_ge(s_m4.h, m4_hist[4 * (li // 2 - 2) + 2 * wi + 1])
                cp = sc.copy(wT[:, (2 * par + wi) * DM:(2 * par + wi + 1) * DM],
                             tq_ps[:, wi * 2048:(wi + 1) * 2048])
                s_we.inc(cp, 1)
            we_hist.append(s_we.n)

        for wi in range(2):
            for tg in range(2):
                bank = 2 * wi + tg

                @blk.tensor
                def _(t, hc=hc, wi=wi, tg=tg, par=par, bank=bank):
                    t.wait_ge(s_we.h, we_hist[2 * hc + wi])
                    if hc >= 1:
                        t.wait_ge(s_hv.h, hv_hist[2 * (hc - 1) + tg])
                    for k in range(16):
                        mm = t.matmul(x_ps[:, bank * 512:(bank + 1) * 512],
                                      wT[:, (2 * par + wi) * DM + k * 128:
                                         (2 * par + wi) * DM + (k + 1) * 128],
                                      xn_o[:, k * OWN + tg * 512: k * OWN + (tg + 1) * 512],
                                      start=(k == 0), stop=(k == 15))
                    t.drain()
                    s_m4.inc(t.nop(), 1)
                m4_hist.append(s_m4.n)

        for tg in range(2):
            @blk.scalar
            def _(sc, hc=hc, tg=tg):
                sc.wait_ge(s_m4.h, m4_hist[4 * hc + tg])
                s_hv.wait(sc)
                a = sc.activation(sgh[:, :], x_ps[:, tg * 512:(tg + 1) * 512],
                                  AF.Sigmoid)
                s_sg.inc(a, 1)

            @blk.vector
            def _(v, hc=hc, tg=tg):
                s_sg.wait(v)
                v.wait_ge(s_m4.h, m4_hist[4 * hc + 2 + tg])
                hh = v.tensor_tensor(HIDB[:, hc * OWN + tg * 512: hc * OWN + (tg + 1) * 512],
                                     sgh[:, :], x_ps[:, 1024 + tg * 512:1024 + (tg + 1) * 512],
                                     AluOpType.mult)
                s_hv.inc(hh, 1)
            hv_hist.append(s_hv.n)

    @blk.sync
    def _(s):
        if "hid" in tap_t:
            s_hv.wait(s)
            s_tap.inc(s.dma_start(tap_t["hid"][:, :], HIDB[:, :]), 16)

    ph4a.close()
    open_stacks.remove(ph4a)

    if stop_after == "ph4a":
        return finish()

    ph4b = ExitStack()
    open_stacks.append(ph4b)
    w3b = ph4b.enter_context(nc.sbuf_tensor("w3b", [128, HIDN], F32))
    qza3 = ph4b.enter_context(nc.sbuf_tensor("qza3", [128, HIDN], BF16))
    qzb3 = ph4b.enter_context(nc.sbuf_tensor("qzb3", [128, HIDN], BF16))
    qzw3 = ph4b.enter_context(nc.sbuf_tensor("qzw3", [128, HIDN], BF16))
    w3T = ph4b.enter_context(nc.sbuf_tensor("w3T", [128, 2 * HIDN], BF16))
    gch = ph4b.enter_context(nc.sbuf_tensor("gch", [128, 512], BF16))
    ssb = ph4b.enter_context(nc.sbuf_tensor("ssb", [128, 2 * OWN], BF16))
    sgo = ph4b.enter_context(nc.sbuf_tensor("sgo", [128, 2 * OWN], BF16))
    t3_ps = ph4b.enter_context(nc.psum_tensor("t3_ps", [128, 4096], BF16))
    f_ps = ph4b.enter_context(nc.psum_tensor("f_ps", [128, 1024], F32))

    b_qz, b_wt, b_we, b_m4, b_sg, b_fs, b_sgd = [], [], [], [], [], [], []

    @blk.scalar
    def _(sc):
        sc.preload_act_table(AF.Silu)

    for dc in range(16):
        par = dc % 2

        @blk.gpsimd
        def _(s, dc=dc, par=par):
            if dc == 0:
                s_m4.wait(s)
                s_hv.wait(s)
            if dc >= 1:
                s.wait_ge(s_qz.h, b_qz[dc - 1])
            s_f.inc(s.dma_start(w3b[:, :], w3[dc * 128:(dc + 1) * 128, :]), 16)
            if dc >= 2:
                s.wait_ge(s_sg.h, b_sg[dc - 2])
            s_fs.inc(s.dma_start(ssb[:, par * OWN:(par + 1) * OWN],
                                 bass.AP(s_d, dc * 128 * OWN, [[OWN, 128], [1, OWN]])), 16)
        b_fs.append(s_fs.n)

        @blk.vector
        def _(v, dc=dc):
            s_f.wait(v)
            s_ga.wait(v)
            if dc >= 1:
                v.wait_ge(s_wt.h, b_wt[dc - 1])
            v.tensor_scalar(qza3[:, :], w3b[:, :], thr_sb[:, 2:3], None, AluOpType.is_ge)
            v.tensor_scalar(qzb3[:, :], w3b[:, :], -1.0, thr_sb[:, 2:3],
                            AluOpType.mult, AluOpType.is_ge)
            q = v.tensor_tensor(qzw3[:, :], qza3[:, :], qzb3[:, :], AluOpType.subtract)
            s_qz.inc(q, 1)
        b_qz.append(s_qz.n)

        for rnd in range(3):
            lo, hi = 16 * rnd, min(16 * rnd + 16, NHC)
            ri = 3 * dc + rnd

            @blk.tensor
            def _(t, rnd=rnd, lo=lo, hi=hi, ri=ri):
                s_qz.wait(t)
                if ri < 2:
                    t.wait_ge(s_we.h, 2 * NHC)
                    s_hv.wait(t)
                if ri >= 2:
                    t.wait_ge(s_we.h, b_we[ri - 2])
                for k in range(lo, hi):
                    mm = t.transpose(
                        t3_ps[:, (rnd % 2) * 2048 + (k - lo) * 128:
                              (rnd % 2) * 2048 + (k - lo + 1) * 128],
                        qzw3[:, k * 128:(k + 1) * 128], ident[:, :])
                t.drain()
                s_wt.inc(t.nop(), 1)

            @blk.scalar
            def _(sc, dc=dc, rnd=rnd, lo=lo, hi=hi, par=par):
                s_wt.wait(sc)
                if dc >= 2:
                    sc.wait_ge(s_m4.h, b_m4[2 * (dc - 2) + 1])
                cp = sc.copy(w3T[:, par * HIDN + lo * 128: par * HIDN + hi * 128],
                             t3_ps[:, (rnd % 2) * 2048:(rnd % 2) * 2048 + (hi - lo) * 128])
                s_we.inc(cp, 1)
            b_we.append(s_we.n)
        b_wt.append(s_wt.n)

        for tg in range(2):
            @blk.tensor
            def _(t, dc=dc, tg=tg, par=par):
                t.wait_ge(s_we.h, b_we[3 * dc + 2])
                if dc == 0:
                    s_hv.wait(t)
                    t.wait_ge(s_sg.h, 2 * NHC)
                if dc >= 1:
                    t.wait_ge(s_sg.h, b_sg[dc - 1])
                for k in range(NHC):
                    mm = t.matmul(f_ps[:, tg * 512:(tg + 1) * 512],
                                  w3T[:, par * HIDN + k * 128: par * HIDN + (k + 1) * 128],
                                  HIDB[:, k * OWN + tg * 512: k * OWN + (tg + 1) * 512],
                                  start=(k == 0), stop=(k == NHC - 1))
                t.drain()
                s_m4.inc(t.nop(), 1)
            b_m4.append(s_m4.n)

            @blk.scalar
            def _(sc, tg=tg):
                s_m4.wait(sc)
                s_sg.wait(sc)
                a = sc.activation(gch[:, :], f_ps[:, tg * 512:(tg + 1) * 512], AF.Silu)
                s_sg.inc(a, 1)

            @blk.vector
            def _(v, dc=dc, tg=tg, par=par):
                s_sg.wait(v)
                s_fs.wait(v, b_fs[dc])
                if dc >= 2:
                    v.wait_ge(s_sgd.h, b_sgd[dc - 2])
                m = v.tensor_tensor(sgo[:, par * OWN + tg * 512: par * OWN + (tg + 1) * 512],
                                    gch[:, :],
                                    ssb[:, par * OWN + tg * 512: par * OWN + (tg + 1) * 512],
                                    AluOpType.mult)
                s_sg.inc(m, 1)
        b_sg.append(s_sg.n)

        @blk.sync
        def _(s, dc=dc, par=par):
            s_sg.wait(s)
            dma = s.dma_start(bass.AP(sg_d, dc * 128 * OWN, [[OWN, 128], [1, OWN]]),
                              sgo[:, par * OWN:(par + 1) * OWN])
            s_sgd.inc(dma, 16)
        b_sgd.append(s_sgd.n)

    @blk.sync
    def _(s):
        if "sg" in tap_t:
            s_sgd.wait(s)
            s_tap.inc(s.dma_start(tap_t["sg"][:, :], sg_d[:, :]), 16)

    ph4b.close()
    open_stacks.remove(ph4b)

    if stop_after == "ph4b":
        return finish()

    phf = ExitStack()
    open_stacks.append(phf)
    xo_sb = phf.enter_context(nc.sbuf_tensor("xo_sb", [128, 2 * DM], F32))
    sgf = phf.enter_context(nc.sbuf_tensor("sgf", [128, 2 * DM], BF16))
    ou_sb = phf.enter_context(nc.sbuf_tensor("ou_sb", [128, 2 * DM], F32))
    fT_ps = phf.enter_context(nc.psum_tensor("fT_ps", [128, 4096], BF16))

    fo_hist, out_hist = [], []
    for tb in range(8):
        par = tb % 2

        @blk.gpsimd
        def _(s, tb=tb, par=par):
            if tb == 0:
                s_sgd.wait(s)
            if tb >= 2:
                s.wait_ge(s_fo.h, fo_hist[tb - 2])
            s_fx.inc(s.dma_start(xo_sb[:, par * DM:(par + 1) * DM],
                                 x_own[tb * 128:(tb + 1) * 128, :]), 16)
            src = bass.AP(sg_d, tb * 128, [[OWN, 128], [128 * OWN, 16], [1, 128]])
            s_fx.inc(s.dma_start(sgf[:, par * DM:(par + 1) * DM], src), 16)

        @blk.tensor
        def _(t, tb=tb, par=par):
            s_fx.wait(t)
            if tb >= 2:
                t.wait_ge(s_fo.h, fo_hist[tb - 2])
            for dc2 in range(16):
                mm = t.transpose(fT_ps[:, par * 2048 + dc2 * 128: par * 2048 + (dc2 + 1) * 128],
                                 sgf[:, par * DM + dc2 * 128: par * DM + (dc2 + 1) * 128],
                                 ident[:, :])
            t.drain()
            s_ft.inc(t.nop(), 1)

        @blk.vector
        def _(v, tb=tb, par=par):
            s_ft.wait(v)
            if tb >= 2:
                v.wait_ge(s_out.h, out_hist[tb - 2])
            o = v.scalar_tensor_tensor(ou_sb[:, par * DM:(par + 1) * DM],
                                       fT_ps[:, par * 2048:(par + 1) * 2048], 1.0,
                                       xo_sb[:, par * DM:(par + 1) * DM],
                                       AluOpType.mult, AluOpType.add)
            s_fo.inc(o, 1)
        fo_hist.append(s_fo.n)

        @blk.sync
        def _(s, tb=tb, par=par):
            s_fo.wait(s)
            dma = s.dma_start(out_t[tb * 128:(tb + 1) * 128, :],
                              ou_sb[:, par * DM:(par + 1) * DM])
            s_out.inc(dma, 16)
        out_hist.append(s_out.n)

    phf.close()
    open_stacks.remove(phf)
    return finish()



_NC_CACHE = {}


def get_nc(taps=(), stop_after="full"):
    key = (tuple(sorted(taps)), stop_after)
    if key not in _NC_CACHE:
        _NC_CACHE[key] = build_nc(taps, stop_after)
    return _NC_CACHE[key]


def make_in_maps(inputs):
    x = np.ascontiguousarray(inputs["L0_attn_x"], dtype=np.float32)
    shared = {
        "qkv_w": np.ascontiguousarray(inputs["L0_attn_qkv_w"], np.float32),
        "dense_w": np.ascontiguousarray(inputs["L0_attn_dense_w"], np.float32),
        "w1": np.ascontiguousarray(inputs["L0_ffn_w1"], np.float32),
        "w2": np.ascontiguousarray(inputs["L0_ffn_w2"], np.float32),
        "w3": np.ascontiguousarray(inputs["L0_ffn_w3"], np.float32),
        "rel_t": np.ascontiguousarray(inputs["L0_attn_rel_attn_table"], np.float32),
    }
    consts = {0: _host_constants(0), 1: _host_constants(1)}
    maps = []
    for c in range(8):
        b, half = divmod(c, 2)
        xt = x[b].reshape(NT, 128, DM)
        perm = [(k + half if k % 2 == 0 else k - half) for k in range(NT)]
        m = dict(shared)
        m["x_loc"] = np.ascontiguousarray(xt[perm].reshape(TOK, DM))
        m["x_own"] = np.ascontiguousarray(
            xt[[2 * i + half for i in range(8)]].reshape(OWN, DM))
        cc = consts[half]
        for k in ("oh_e", "oh_o", "cq", "sq", "ck", "sk", "ident", "jmat",
                  "pswap", "ones_f"):
            m[k] = cc[k]
        maps.append(m)
    return maps


def kernel(**inputs):
    nc = get_nc()
    maps = make_in_maps(inputs)
    res = run_bass_kernel_spmd(nc, maps, core_ids=list(range(8)))
    out = np.empty((4, TOK, DM), np.float32)
    for c in range(8):
        b, half = divmod(c, 2)
        o = res.results[c]["out"].reshape(8, 128, DM)
        ot = out[b].reshape(NT, 128, DM)
        for i in range(8):
            ot[2 * i + half] = o[i]
    return out


# version 05
# speedup vs baseline: 1.0344x; 1.0344x over previous
import math
from contextlib import ExitStack

import numpy as np
import ml_dtypes

import concourse.bass as bass
import concourse.mybir as mybir
from concourse.alu_op_type import AluOpType
from concourse.bass_utils import run_bass_kernel_spmd

F32 = mybir.dt.float32
BF16 = mybir.dt.bfloat16
AF = mybir.ActivationFunctionType
AX = mybir.AxisListType.X
BFD = ml_dtypes.bfloat16

NEG = -30000.0
NH, HD, DM = 16, 128, 2048
HIDN = 5632
NHC = HIDN // 128
TOK, OWN, NT = 2048, 1024, 16
C0, C1 = 1792, 1664
NG = 2048
W_E = 1920
NWEL = HIDN * DM
GCH = 2048
NWCH = NWEL // (128 * GCH)


def _bucket_delta(d):
    n = np.asarray(np.maximum(d, 0), np.int64)
    nf = np.maximum(n, 1).astype(np.float32)
    large = (np.log(nf / np.float32(16.0)) / np.float32(math.log(8.0))
             * np.float32(16.0)).astype(np.int32) + 16
    large = np.minimum(large, 31)
    return np.where(n < 16, n, large).astype(np.int64)


def _host_constants(half):
    c = {}
    n = np.arange(NG)
    for name, dd in (("oh_e", C0 + 127 - n), ("oh_o", C1 + 127 + 256 * half - n)):
        oh = np.zeros((33, NG), np.float32)
        valid = (dd >= 0) & (dd <= 2047)
        bk = _bucket_delta(np.maximum(dd, 0))
        oh[bk[valid], n[valid]] = 1.0
        oh[32, ~valid] = 1.0
        c[name] = oh
    inv = (1.0 / np.power(np.float32(10000.0),
                          np.arange(0, HD, 2, dtype=np.float32) / np.float32(HD))
           ).astype(np.float32)
    kap = np.arange(NT)
    m = np.where(kap % 2 == 0, kap + half, kap - half)
    pos_loc = (m[:, None] * 128 + np.arange(128)[None, :]).reshape(-1).astype(np.float32)
    pos_own = ((2 * np.arange(8)[:, None] + half) * 128
               + np.arange(128)[None, :]).reshape(-1).astype(np.float32)

    def tabs(pos, scale):
        ang = pos[None, :] * inv[:, None]
        cos = (np.cos(ang) * scale).astype(np.float32)
        sin = (np.sin(ang) * scale).astype(np.float32)
        return (np.concatenate([cos, cos], 0).astype(BFD),
                np.concatenate([-sin, sin], 0).astype(BFD))

    c["cq"], c["sq"] = tabs(pos_own, np.float32(1.0 / math.sqrt(HD)))
    c["ck"], c["sk"] = tabs(pos_loc, np.float32(1.0))
    c["ident"] = np.eye(128, dtype=BFD)
    jm = np.zeros((128, 128), np.float32)
    jm[np.arange(128), 127 - np.arange(128)] = 1.0
    c["jmat"] = jm.astype(BFD)
    pw = np.zeros((128, 128), np.float32)
    pw[(np.arange(128) + 64) % 128, np.arange(128)] = 1.0
    c["pswap"] = pw.astype(BFD)
    c["ones_f"] = np.ones((128, 128), np.float32)
    return c


class Sem:
    def __init__(self, ctx, nc, name):
        self.h = ctx.enter_context(nc.semaphore(name))
        self.n = 0

    def inc(self, instr, k=1):
        instr.then_inc(self.h, k)
        self.n += k
        return instr

    def wait(self, eng, val=None):
        v = self.n if val is None else val
        if v > 0:
            eng.wait_ge(self.h, v)


SERIAL = 0

def build_nc(taps=(), stop_after="full"):
    taps = set(taps)
    nc = bass.Bass("TRN2", target_bir_lowering=False)

    def din(name, shape, dt=F32):
        return nc.dram_tensor(name, shape, dt, kind="ExternalInput")

    x_loc = din("x_loc", [TOK, DM])
    x_own = din("x_own", [OWN, DM])
    qkv_w = din("qkv_w", [DM, 3 * DM])
    dense_w = din("dense_w", [DM, DM])
    w1 = din("w1", [HIDN, DM])
    w2 = din("w2", [HIDN, DM])
    w3 = din("w3", [DM, HIDN])
    rel_t = din("rel_t", [32, NH])
    oh_e = din("oh_e", [33, NG])
    oh_o = din("oh_o", [33, NG])
    cq_t = din("cq", [128, OWN], BF16)
    sq_t = din("sq", [128, OWN], BF16)
    ck_t = din("ck", [128, TOK], BF16)
    sk_t = din("sk", [128, TOK], BF16)
    ident_t = din("ident", [128, 128], BF16)
    jmat_t = din("jmat", [128, 128], BF16)
    pswap_t = din("pswap", [128, 128], BF16)
    ones_t = din("ones_f", [128, 128])

    out_t = nc.dram_tensor("out", [OWN, DM], F32, kind="ExternalOutput")
    tap_shapes = {
        "xn": ([DM, TOK], BF16), "qs": ([NH * 128, OWN], BF16),
        "kd": ([NH * 128, TOK], BF16), "vd": ([NH * TOK, HD], BF16),
        "gd": ([2 * NH, NG], F32), "ctx": ([NH * 128, OWN], BF16),
        "sd": ([DM, OWN], BF16), "thr": ([128, 4], F32),
        "hid": ([128, NHC * OWN], BF16), "sg": ([DM, OWN], BF16),
        "p00": ([128, 4352], BF16), "s00": ([128, 64], F32),
    }
    tap_t = {tp: nc.dram_tensor("tap_" + tp, *tap_shapes[tp], kind="ExternalOutput")
             for tp in taps if tp in tap_shapes}

    g_e_d = nc.dram_tensor("g_e_d", [NH, NG], BF16)
    g_o_d = nc.dram_tensor("g_o_d", [NH, NG], BF16)
    k_d = nc.dram_tensor("k_d", [NH * 128, TOK], BF16)
    v_d = nc.dram_tensor("v_d", [NH * TOK, HD], BF16)
    xn_d = nc.dram_tensor("xn_d", [DM, OWN], BF16)
    s_d = nc.dram_tensor("s_d", [DM, OWN], BF16)
    sg_d = nc.dram_tensor("sg_d", [DM, OWN], BF16)

    ctx = ExitStack()
    E = ctx.enter_context

    ident = E(nc.sbuf_tensor("c_ident", [128, 128], BF16))
    jmat = E(nc.sbuf_tensor("c_jmat", [128, 128], BF16))
    pswap = E(nc.sbuf_tensor("c_pswap", [128, 128], BF16))
    ones = E(nc.sbuf_tensor("c_ones", [128, 128], F32))
    thr_sb = E(nc.sbuf_tensor("thr_sb", [128, 4], F32))
    epsb = E(nc.sbuf_tensor("c_eps", [128, 1], F32))

    names = ["s_c", "s_x", "s_g", "s_ln", "s_sq", "s_tp", "s_xe", "s_xnd",
             "s_w", "s_mq", "s_eq", "s_rq", "s_kv", "s_gm", "s_ga",
             "s_h", "s_sc", "s_mx", "s_ex", "s_dm", "s_pt", "s_pe", "s_pv",
             "s_cx", "s_d3", "s_de", "s_sd", "s_f", "s_qz", "s_wt", "s_we",
             "s_m4", "s_sg", "s_hv", "s_fs", "s_sgd", "s_fx", "s_ft", "s_fo",
             "s_out", "s_tap", "s_vch", "s_sw", "s_ach"]
    S = {nm: Sem(ctx, nc, nm) for nm in names}
    (s_c, s_x, s_g, s_ln, s_sq, s_tp, s_xe, s_xnd, s_w, s_mq, s_eq, s_rq,
     s_kv, s_gm, s_ga, s_h, s_sc, s_mx, s_ex, s_dm, s_pt, s_pe, s_pv, s_cx,
     s_d3, s_de, s_sd, s_f, s_qz, s_wt, s_we, s_m4, s_sg, s_hv, s_fs, s_sgd,
     s_fx, s_ft, s_fo, s_out, s_tap, s_vch, s_sw, s_ach) = (S[nm] for nm in names)

    blk = E(nc.Block())

    open_stacks = []

    def finish():
        @blk.sync
        def _(s):
            s_out.wait(s)
            s_tap.wait(s)
        for st in reversed(open_stacks):
            st.close()
        ctx.close()
        return nc

    phA = ExitStack()
    open_stacks.append(phA)
    xn_fm = phA.enter_context(nc.sbuf_tensor("xn_fm", [128, NT * DM], BF16))
    QS = phA.enter_context(nc.sbuf_tensor("QS", [128, NH * OWN], BF16))

    phR = ExitStack()
    open_stacks.append(phR)
    cq_sb = phR.enter_context(nc.sbuf_tensor("cq_sb", [128, OWN], BF16))
    sq_sb = phR.enter_context(nc.sbuf_tensor("sq_sb", [128, OWN], BF16))
    ck_sb = phR.enter_context(nc.sbuf_tensor("ck_sb", [128, TOK], BF16))
    sk_sb = phR.enter_context(nc.sbuf_tensor("sk_sb", [128, TOK], BF16))

    ph0 = ExitStack()
    open_stacks.append(ph0)
    tab_aug = ph0.enter_context(nc.sbuf_tensor("tab_aug", [33, NH], F32))
    oh_sb = ph0.enter_context(nc.sbuf_tensor("oh_sb", [33, 2 * NG], F32))
    g_sb = ph0.enter_context(nc.sbuf_tensor("g_sb", [NH, 2 * NG], F32))
    xbuf = ph0.enter_context(nc.sbuf_tensor("xbuf", [128, 2 * DM], F32))
    sqj = ph0.enter_context(nc.sbuf_tensor("sqj", [128, DM], BF16))
    xnt = ph0.enter_context(nc.sbuf_tensor("xnt", [128, 2 * DM], BF16))
    st0 = ph0.enter_context(nc.sbuf_tensor("st0", [128, 64], F32))
    g_ps = ph0.enter_context(nc.psum_tensor("g_ps", [128, NG], F32))
    tp_ps = ph0.enter_context(nc.psum_tensor("tp_ps", [128, 2048], BF16))

    @blk.sync
    def _(s):
        for dst, src in ((ident, ident_t), (jmat, jmat_t), (pswap, pswap_t),
                         (ones, ones_t),
                         (cq_sb, cq_t), (sq_sb, sq_t), (ck_sb, ck_t), (sk_sb, sk_t)):
            s_c.inc(s.dma_start(dst[:, :], src[:, :]), 16)
        s_c.inc(s.dma_start(tab_aug[0:32, :], rel_t[:, :]), 16)
        s_c.inc(s.dma_start(oh_sb[:, 0:NG], oh_e[:, :]), 16)
        s_c.inc(s.dma_start(oh_sb[:, NG:], oh_o[:, :]), 16)

    @blk.scalar
    def _(sc):
        sc.preload_act_table(AF.Sqrt)

    @blk.vector
    def _(v):
        s_g.inc(v.memset(epsb[:, :], 1e-5), 1)
        s_g.inc(v.memset(tab_aug[32:33, :], NEG), 1)

    for rnd in range(2):
        @blk.tensor
        def _(t, rnd=rnd):
            s_c.wait(t)
            s_g.wait(t)
            for j in range(4):
                mm = t.matmul(g_ps[0:NH, 512 * j:512 * (j + 1)], tab_aug[:, :],
                              oh_sb[:, rnd * NG + 512 * j:rnd * NG + 512 * (j + 1)],
                              start=True, stop=True)
            t.drain()
            s_g.inc(t.nop(), 1)

        @blk.vector
        def _(v, rnd=rnd):
            s_g.wait(v)
            cp = v.tensor_copy(g_sb[:, rnd * NG:(rnd + 1) * NG], g_ps[0:NH, :])
            s_g.inc(cp, 1)

    @blk.gpsimd
    def _(g):
        s_g.wait(g)
        s_g.inc(g.dma_start(g_e_d[:, :], g_sb[:, 0:NG]), 16)
        s_g.inc(g.dma_start(g_o_d[:, :], g_sb[:, NG:]), 16)
        if "gd" in tap_t:
            s_g.wait(g)
            s_tap.inc(g.dma_start(tap_t["gd"][0:NH, :], g_sb[:, 0:NG]), 16)
            s_tap.inc(g.dma_start(tap_t["gd"][NH:, :], g_sb[:, NG:]), 16)

    ln_done = []
    for t_i in range(NT):
        par = t_i % 2
        xb = xbuf[:, par * DM:(par + 1) * DM]
        xo = xnt[:, par * DM:(par + 1) * DM]
        sb = par * 32

        @blk.gpsimd
        def _(s, t_i=t_i, xb=xb):
            if t_i >= 2:
                s.wait_ge(s_ln.h, ln_done[t_i - 2])
            s_x.inc(s.dma_start(xb, x_loc[t_i * 128:(t_i + 1) * 128, :]), 16)

        @blk.scalar
        def _(sc, t_i=t_i, xb=xb, sb=sb):
            s_x.wait(sc)
            if t_i >= 2:
                sc.wait_ge(s_ln.h, ln_done[t_i - 2])
            s_ach.inc(sc.activation(sqj[:, :], xb, AF.Square,
                                    accum_out=st0[:, sb + 16:sb + 17]), 1)
            s_ach.wait(sc)
            s_sq.inc(sc.nop(), 1)

        @blk.vector
        def _(v, xb=xb, sb=sb):
            s_x.wait(v)
            s_vch.inc(v.reduce_sum(st0[:, sb:sb + 1], xb, axis=AX), 1)
            s_vch.wait(v)
            s_sq.wait(v)
            s_vch.inc(v.tensor_scalar(st0[:, sb + 4:sb + 5], st0[:, sb:sb + 1],
                                      1.0 / DM, None, AluOpType.mult), 1)
            s_vch.wait(v)
            s_vch.inc(v.tensor_tensor(st0[:, sb + 8:sb + 9], st0[:, sb + 4:sb + 5],
                                      st0[:, sb + 4:sb + 5], AluOpType.mult), 1)
            s_vch.wait(v)
            st = v.scalar_tensor_tensor(st0[:, sb + 12:sb + 13], st0[:, sb + 16:sb + 17],
                                        1.0 / DM, st0[:, sb + 8:sb + 9],
                                        AluOpType.mult, AluOpType.subtract)
            s_vch.inc(st, 1)
            s_vch.wait(v)
            s_ln.inc(v.nop(), 1)

        @blk.scalar
        def _(sc, sb=sb):
            s_ln.wait(sc)
            sc.wait_ge(s_g.h, 1)
            s_ach.inc(sc.activation(st0[:, sb + 20:sb + 21], st0[:, sb + 12:sb + 13],
                                    AF.Sqrt, bias=epsb[:, 0:1]), 1)
            s_ach.wait(sc)
            s_sq.inc(sc.nop(), 1)

        @blk.vector
        def _(v, xb=xb, xo=xo, sb=sb):
            s_sq.wait(v)
            s_vch.inc(v.reciprocal(st0[:, sb + 24:sb + 25], st0[:, sb + 20:sb + 21]), 1)
            s_vch.wait(v)
            cp = v.tensor_scalar(xo, xb, st0[:, sb + 4:sb + 5], st0[:, sb + 24:sb + 25],
                                 AluOpType.subtract, AluOpType.mult)
            s_ln.inc(cp, 1)
        ln_done.append(s_ln.n)

        for grp in range(2):
            gl = 2 * t_i + grp

            @blk.tensor
            def _(t, grp=grp, gl=gl, xo=xo):
                s_ln.wait(t)
                if gl >= 2:
                    t.wait_ge(s_xe.h, gl - 1)
                for fc in range(8 * grp, 8 * grp + 8):
                    mm = t.transpose(
                        tp_ps[:, (gl % 2) * 1024 + (fc % 8) * 128:
                              (gl % 2) * 1024 + (fc % 8 + 1) * 128],
                        xo[:, fc * 128:(fc + 1) * 128], ident[:, :])
                t.drain()
                s_tp.inc(t.nop(), 1)

            @blk.vector
            def _(v, t_i=t_i, grp=grp, gl=gl):
                s_tp.wait(v)
                dst = bass.AP(xn_fm, (8 * grp) * DM + t_i * 128,
                              [[NT * DM, 128], [DM, 8], [1, 128]])
                cp = v.tensor_copy(dst, tp_ps[:, (gl % 2) * 1024:(gl % 2 + 1) * 1024])
                s_xe.inc(cp, 1)

    @blk.scalar
    def _(sc):
        sc.preload_act_table(AF.Exp)

    @blk.sync
    def _(s):
        s_xe.wait(s)
        for k in range(16):
            src = bass.AP(xn_fm, k * DM, [[NT * DM, 128], [256, 8], [1, 128]])
            s_xnd.inc(s.dma_start(xn_d[k * 128:(k + 1) * 128, :], src), 16)
        if "xn" in tap_t:
            for k in range(16):
                s_tap.inc(s.dma_start(tap_t["xn"][k * 128:(k + 1) * 128, :],
                                      xn_fm[:, k * DM:(k + 1) * DM]), 16)

    ph0.close()
    open_stacks.remove(ph0)

    if stop_after == "ph0":
        return finish()

    ph1 = ExitStack()
    open_stacks.append(ph1)
    wstr = ph1.enter_context(nc.sbuf_tensor("wstr", [128, 2 * 8192], BF16))
    eh = ph1.enter_context(nc.sbuf_tensor("eh", [128, 2 * 512], BF16))
    rof = ph1.enter_context(nc.sbuf_tensor("rof", [128, 2 * 512], BF16))
    ub = ph1.enter_context(nc.sbuf_tensor("ub", [128, 2 * 512], BF16))
    ko_sb = ph1.enter_context(nc.sbuf_tensor("ko_sb", [128, 4 * 512], BF16))
    vo_sb = ph1.enter_context(nc.sbuf_tensor("vo_sb", [128, 4 * 512], BF16))
    qk_ps = ph1.enter_context(nc.psum_tensor("qk_ps", [128, 2048], F32))
    sw_ps = ph1.enter_context(nc.psum_tensor("sw_ps", [128, 1024], F32))

    NSTR = 12
    mq_done, eq_hist, rq_hist, kv_hist = [], [], [], []
    rope_pend = []

    for si in range(NSTR):
        @blk.gpsimd
        def _(g, si=si):
            if si >= 2:
                g.wait_ge(s_mq.h, mq_done[si - 2])
            else:
                g.wait_ge(s_xe.h, 32)
                s_g.wait(g)
            src = bass.AP(qkv_w, si * 512,
                          [[3 * DM, 128], [128 * 3 * DM, 16], [1, 512]])
            s_w.inc(g.dma_start(wstr[:, (si % 2) * 8192:(si % 2 + 1) * 8192], src), 16)

        wb = (si % 2) * 8192
        if si < 8:
            groups = [(oc4, tg) for oc4 in range(4) for tg in range(2 if si < 4 else 4)]
        else:
            groups = [(tb,) for tb in range(16)]

        def emit_swap(qg):
            @blk.tensor
            def _(t, qg=qg):
                t.wait_ge(s_eq.h, eq_hist[qg])
                if qg >= 2:
                    t.wait_ge(s_rq.h, qg - 1)
                mm = t.matmul(sw_ps[:, (qg % 2) * 512:(qg % 2 + 1) * 512],
                              pswap[:, :], eh[:, (qg % 2) * 512:(qg % 2 + 1) * 512],
                              start=True, stop=True)
                t.drain()
                s_sw.inc(t.nop(), 1)

        def emit_rope(qg, qsi, qdesc):
            oc4, tg = qdesc

            @blk.vector
            def _(v, qg=qg, qsi=qsi, oc4=oc4, tg=tg):
                s_sw.wait(v, qg + 1)
                ee = eh[:, (qg % 2) * 512:(qg % 2 + 1) * 512]
                ro = rof[:, (qg % 2) * 512:(qg % 2 + 1) * 512]
                uu = ub[:, (qg % 2) * 512:(qg % 2 + 1) * 512]
                Ct, St = (cq_sb, sq_sb) if qsi < 4 else (ck_sb, sk_sb)
                co = tg * 512
                v.tensor_tensor(ro, ee, Ct[:, co:co + 512], AluOpType.mult)
                v.tensor_tensor(uu, sw_ps[:, (qg % 2) * 512:(qg % 2 + 1) * 512],
                                St[:, co:co + 512], AluOpType.mult)
                if qsi < 4:
                    h = qsi * 4 + oc4
                    dst = QS[:, h * OWN + co: h * OWN + co + 512]
                else:
                    kg = qg - 32
                    if kg >= 4:
                        v.wait_ge(s_kv.h, kv_hist[kg - 4])
                    dst = ko_sb[:, (kg % 4) * 512:(kg % 4 + 1) * 512]
                fin = v.tensor_tensor(dst, ro, uu, AluOpType.add)
                s_rq.inc(fin, 1)
            rq_hist.append(s_rq.n)

            if qsi >= 4:
                @blk.sync
                def _(s, qg=qg, qsi=qsi, oc4=oc4, tg=tg):
                    h = (qsi - 4) * 4 + oc4
                    s_rq.wait(s)
                    kg = qg - 32
                    dma = s.dma_start(
                        bass.AP(k_d, (h * 128) * TOK + tg * 512,
                                [[TOK, 128], [1, 512]]),
                        ko_sb[:, (kg % 4) * 512:(kg % 4 + 1) * 512])
                    s_kv.inc(dma, 16)
                kv_hist.append(s_kv.n)

        for gdesc in groups:
            gidx = len(eq_hist)
            bank = gidx % 4
            if si == 8 and gdesc == (0,):
                while rope_pend:
                    qg, qsi, qdesc = rope_pend.pop(0)
                    emit_swap(qg)
                    emit_rope(qg, qsi, qdesc)

            @blk.tensor
            def _(t, si=si, gdesc=gdesc, gidx=gidx, bank=bank, wb=wb):
                if gidx == 0:
                    s_xe.wait(t)
                t.wait_ge(s_w.h, 16 * (si + 1))
                if gidx >= 4:
                    t.wait_ge(s_eq.h, gidx - 3)
                for k in range(16):
                    if si < 4:
                        oc4, tg = gdesc
                        rhs = bass.AP(xn_fm, k * DM + tg * 1024,
                                      [[NT * DM, 128], [256, 4], [1, 128]])
                        lhsT = wstr[:, wb + k * 512 + oc4 * 128:
                                    wb + k * 512 + oc4 * 128 + 128]
                    elif si < 8:
                        oc4, tg = gdesc
                        rhs = xn_fm[:, k * DM + tg * 512: k * DM + (tg + 1) * 512]
                        lhsT = wstr[:, wb + k * 512 + oc4 * 128:
                                    wb + k * 512 + oc4 * 128 + 128]
                    else:
                        (tb,) = gdesc
                        lhsT = xn_fm[:, k * DM + tb * 128: k * DM + (tb + 1) * 128]
                        rhs = wstr[:, wb + k * 512: wb + (k + 1) * 512]
                    mm = t.matmul(qk_ps[:, bank * 512:(bank + 1) * 512], lhsT, rhs,
                                  start=(k == 0), stop=(k == 15))
                t.drain()
                s_mq.inc(t.nop(), 1)

            if si < 8:
                @blk.scalar
                def _(sc, gidx=gidx, bank=bank):
                    s_mq.wait(sc)
                    if gidx >= 2:
                        sc.wait_ge(s_rq.h, rq_hist[gidx - 2] if gidx - 2 < len(rq_hist)
                                   else gidx - 1)
                    cp = sc.copy(eh[:, (gidx % 2) * 512:(gidx % 2 + 1) * 512],
                                 qk_ps[:, bank * 512:(bank + 1) * 512])
                    s_eq.inc(cp, 1)
                eq_hist.append(s_eq.n)
                rope_pend.append((gidx, si, gdesc))
                if len(rope_pend) > 1:
                    qg, qsi, qdesc = rope_pend.pop(0)
                    emit_swap(qg)
                    emit_rope(qg, qsi, qdesc)
            else:
                (tb,) = gdesc
                vg = gidx - 96

                @blk.scalar
                def _(sc, bank=bank, vg=vg):
                    s_mq.wait(sc)
                    if vg >= 4:
                        sc.wait_ge(s_kv.h, kv_hist[64 + vg - 4])
                    cp = sc.copy(vo_sb[:, (vg % 4) * 512:(vg % 4 + 1) * 512],
                                 qk_ps[:, bank * 512:(bank + 1) * 512])
                    s_eq.inc(cp, 1)
                eq_hist.append(s_eq.n)
                rq_hist.append(s_rq.n)

                @blk.sync
                def _(s, si=si, tb=tb, vg=vg):
                    s_eq.wait(s)
                    dst = bass.AP(v_d, ((si - 8) * 4 * TOK + tb * 128) * HD,
                                  [[HD, 128], [TOK * HD, 4], [1, 128]])
                    dma = s.dma_start(dst, vo_sb[:, (vg % 4) * 512:(vg % 4 + 1) * 512])
                    s_kv.inc(dma, 16)
                kv_hist.append(s_kv.n)
        mq_done.append(s_mq.n)

    @blk.sync
    def _(s):
        if "qs" in tap_t:
            s_rq.wait(s)
            for h in range(NH):
                s_tap.inc(s.dma_start(tap_t["qs"][h * 128:(h + 1) * 128, :],
                                      QS[:, h * OWN:(h + 1) * OWN]), 16)
        if "kd" in tap_t:
            s_kv.wait(s)
            s_tap.inc(s.dma_start(tap_t["kd"][:, :], k_d[:, :]), 16)
        if "vd" in tap_t:
            s_kv.wait(s)
            s_tap.inc(s.dma_start(tap_t["vd"][:, :], v_d[:, :]), 16)

    ph1.close()
    open_stacks.remove(ph1)
    phR.close()
    open_stacks.remove(phR)

    if stop_after == "ph1":
        return finish()

    CTX = phA.enter_context(nc.sbuf_tensor("CTX", [128, NH * OWN], BF16))
    ph2 = ExitStack()
    open_stacks.append(ph2)
    khb = ph2.enter_context(nc.sbuf_tensor("khb", [128, 2 * TOK], BF16))
    vhb = ph2.enter_context(nc.sbuf_tensor("vhb", [128, 2 * TOK], BF16))
    ree = ph2.enter_context(nc.sbuf_tensor("ree", [128, 2 * W_E], BF16))
    reo = ph2.enter_context(nc.sbuf_tensor("reo", [128, 2 * W_E], BF16))
    pbf = ph2.enter_context(nc.sbuf_tensor("pbf", [128, 3072], BF16))
    pts = ph2.enter_context(nc.sbuf_tensor("pts", [128, 2 * 512], BF16))
    dmat = ph2.enter_context(nc.sbuf_tensor("dmat", [128, 2 * 128], BF16))
    sst = ph2.enter_context(nc.sbuf_tensor("sst", [128, 64], F32))
    gbuf = ph2.enter_context(nc.sbuf_tensor("gbuf", [128, 3 * GCH], F32))
    gjunk = ph2.enter_context(nc.sbuf_tensor("gjunk", [128, GCH], BF16))
    gpart = ph2.enter_context(nc.sbuf_tensor("gpart", [128, 3 * 192], F32))
    gtot = ph2.enter_context(nc.sbuf_tensor("gtot", [128, 16], F32))
    gth1 = ph2.enter_context(nc.sbuf_tensor("gth1", [1, 4], F32))
    sca = ph2.enter_context(nc.psum_tensor("sca", [128, 1024], F32))
    scb = ph2.enter_context(nc.psum_tensor("scb", [128, 2048], F32))
    cxp = ph2.enter_context(nc.psum_tensor("cxp", [128, 1024], F32))

    wlist = [w1, w2, w3]
    NG_ALL = 3 * NWCH
    gm_hist, ga_hist = [], []

    def emit_gamma_chunk(gi):
        wi, j = divmod(gi, NWCH)

        @blk.gpsimd
        def _(g, gi=gi, wi=wi, j=j):
            if gi >= 3:
                g.wait_ge(s_ga.h, ga_hist[gi - 3])
            if gi < 2:
                s_kv.wait(g)
            src = bass.AP(wlist[wi], j * 128 * GCH, [[GCH, 128], [1, GCH]])
            s_gm.inc(g.dma_start(gbuf[:, (gi % 3) * GCH:(gi % 3 + 1) * GCH], src), 16)
        gm_hist.append(s_gm.n)

        if gi % 2 == 1:
            @blk.vector
            def _(v, gi=gi, wi=wi, j=j):
                s_gm.wait(v, gm_hist[gi])
                r = v.reduce_sum(gpart[:, wi * 192 + 4 * j: wi * 192 + 4 * j + 1],
                                 gbuf[:, (gi % 3) * GCH:(gi % 3 + 1) * GCH],
                                 axis=AX, apply_absolute_value=True)
                s_ga.inc(r, 1)
        else:
            @blk.scalar
            def _(sc, gi=gi, wi=wi, j=j):
                s_gm.wait(sc, gm_hist[gi])
                r = sc.activation(gjunk[:, :],
                                  gbuf[:, (gi % 3) * GCH:(gi % 3 + 1) * GCH],
                                  AF.Abs, accum_out=gpart[:, wi * 192 + 4 * j: wi * 192 + 4 * j + 1])
                s_ach.inc(r, 1)
                s_ach.wait(sc)
                s_ga.inc(sc.nop(), 1)
        ga_hist.append(s_ga.n)

    def klen_of(qt):
        return 256 * (qt + 1)

    pairs = [(0, 4), (1, 5), (2, 6), (3, 7)]
    cx_hist = []
    sc_pair0 = []
    gamma_iter = iter(range(NG_ALL))
    pe_after_qt = {}
    pt_after_qt = {}
    n_ptgroups = 0

    for h in range(NH):
        par = h % 2

        qengine = blk.sync if SERIAL == -8 else blk.scalar

        @qengine
        def _(s, h=h, par=par):
            if h == 0:
                s_kv.wait(s)
                s_g.wait(s)
            if h >= 2:
                s.wait_ge(s_cx.h, cx_hist[h - 2])
            if SERIAL != -9 and h >= 1:
                s.wait_ge(s_sc.h, sc_pair0[h - 1])
            if SERIAL == 4 and h >= 1:
                s.wait_ge(s_cx.h, cx_hist[h - 1])
            if SERIAL == 5 and h >= 1:
                s.wait_ge(s_cx.h, cx_hist[h - 1])
            s_h.inc(s.dma_start(khb[:, par * TOK:(par + 1) * TOK],
                                bass.AP(k_d, h * 128 * TOK, [[TOK, 128], [1, TOK]])), 16)
            if SERIAL == 7 and h >= 1:
                s.wait_ge(s_cx.h, cx_hist[h - 1])
            s_h.inc(s.dma_start(
                vhb[:, par * TOK:(par + 1) * TOK],
                bass.AP(v_d, h * TOK * HD, [[HD, 128], [128 * HD, 16], [1, 128]])), 16)
            if SERIAL == 6 and h >= 1:
                s.wait_ge(s_cx.h, cx_hist[h - 1])
            s_h.inc(s.dma_start(ree[:, par * W_E:(par + 1) * W_E],
                                bass.AP(g_e_d, h * NG, [[1, 128], [1, W_E]])), 16)
            s_h.inc(s.dma_start(reo[:, par * W_E:(par + 1) * W_E],
                                bass.AP(g_o_d, h * NG, [[1, 128], [1, W_E]])), 16)

        if SERIAL != 3:
            for _i in range(9):
                gi = next(gamma_iter, None)
                if gi is not None:
                    emit_gamma_chunk(gi)

        for pi, (qa, qb) in enumerate(pairs):
            if pi == 1:
                sc_pair0.append(s_sc.n)
            dm_snap = {}
            for stream, qt in ((0, qa), (1, qb)):
                ps = sca if stream == 0 else scb
                kl = klen_of(qt)
                nk = kl // 128

                @blk.tensor
                def _(t, h=h, par=par, pi=pi, stream=stream, qt=qt, ps=ps,
                      kl=kl, nk=nk):
                    if pi == 0:
                        t.wait_ge(s_h.h, 64 * (h + 1))
                        if h == 0:
                            t.wait_ge(s_rq.h, 32)
                    if stream in pe_after_qt:
                        t.wait_ge(s_pe.h, pe_after_qt[stream])
                    if SERIAL >= 1:
                        s_pv.wait(t)
                        s_pe.wait(t)
                        s_cx.wait(t)
                    kh = khb[:, par * TOK:(par + 1) * TOK]
                    for c in range((kl + 511) // 512):
                        w = min(512, kl - c * 512)
                        mm = t.matmul(ps[:, c * 512: c * 512 + w],
                                      QS[:, h * OWN + qt * 128: h * OWN + (qt + 1) * 128],
                                      kh[:, c * 512: c * 512 + w], start=True, stop=True)
                    for kap in range(nk):
                        tt = 2 * qt - kap
                        s0 = (C0 if kap % 2 == 0 else C1) - 128 * tt
                        rsb = ree if kap % 2 == 0 else reo
                        mm = t.matmul(ps[:, kap * 128:(kap + 1) * 128], jmat[:, :],
                                      rsb[:, par * W_E + s0: par * W_E + s0 + 128],
                                      start=False, stop=True, skip_group_check=True)
                    t.drain()
                    s_sc.inc(t.nop(), 1)

                @blk.vector
                def _(v, stream=stream, ps=ps, kl=kl):
                    s_sc.wait(v)
                    if stream in pt_after_qt:
                        v.wait_ge(s_pt.h, pt_after_qt[stream])
                    r = v.reduce_max(sst[:, stream * 32:stream * 32 + 1], ps[:, 0:kl],
                                     axis=AX, negate=True)
                    s_vch.inc(r, 1)
                    s_vch.wait(v)
                    s_mx.inc(v.nop(), 1)

                @blk.scalar
                def _(sc, stream=stream, ps=ps, kl=kl):
                    s_mx.wait(sc)
                    po = 0 if stream == 0 else 1024
                    a = sc.activation(pbf[:, po:po + kl], ps[:, 0:kl], AF.Exp,
                                      bias=sst[:, stream * 32:stream * 32 + 1],
                                      accum_out=sst[:, stream * 32 + 4:stream * 32 + 5])
                    s_ach.inc(a, 1)
                    s_ach.wait(sc)
                    s_ex.inc(sc.nop(), 1)

                @blk.vector
                def _(v, stream=stream):
                    s_ex.wait(v)
                    s_vch.inc(v.reciprocal(sst[:, stream * 32 + 8:stream * 32 + 9],
                                           sst[:, stream * 32 + 4:stream * 32 + 5]), 1)
                    s_vch.wait(v)
                    d = v.tensor_scalar(dmat[:, stream * 128:(stream + 1) * 128],
                                        ident[:, :], sst[:, stream * 32 + 8:stream * 32 + 9],
                                        None, AluOpType.mult)
                    s_dm.inc(d, 1)
                dm_snap[stream] = s_dm.n

            if h == 0 and pi == 1 and ("p00" in tap_t):
                @blk.gpsimd
                def _(g):
                    g.wait_ge(s_pv.h, 4)
                    s_tap.inc(g.dma_start(tap_t["p00"][:, 0:3072], pbf[:, :]), 16)
                    s_tap.inc(g.dma_start(tap_t["p00"][:, 3072:3328], dmat[:, :]), 16)
                    s_tap.inc(g.dma_start(tap_t["p00"][:, 3328:4352], pts[:, :]), 16)
                    s_tap.inc(g.dma_start(tap_t["s00"][:, :], sst[:, :]), 16)

                @blk.tensor
                def _(t):
                    s_tap.wait(t)

            for stream, qt in ((0, qa), (1, qb)):
                ps = sca if stream == 0 else scb
                kl = klen_of(qt)
                nk = kl // 128
                nbank = 2 if stream == 0 else 4
                ngrp = (nk + 3) // 4
                poff = 0 if stream == 0 else 1024
                for g in range(ngrp):
                    klo, khi = 4 * g, min(4 * g + 4, nk)
                    gpar = n_ptgroups % 2
                    my_pt_index = n_ptgroups

                    @blk.tensor
                    def _(t, stream=stream, ps=ps, g=g, klo=klo, khi=khi,
                          nbank=nbank, poff=poff, dmv=dm_snap[stream]):
                        if g == 0:
                            t.wait_ge(s_dm.h, dmv)
                        if SERIAL >= 2:
                            s_dm.wait(t)
                            s_ex.wait(t)
                        bank = g % nbank
                        for kk in range(klo, khi):
                            mm = t.matmul(
                                ps[:, bank * 512 + (kk - klo) * 128:
                                   bank * 512 + (kk - klo + 1) * 128],
                                pbf[:, poff + kk * 128: poff + (kk + 1) * 128],
                                dmat[:, stream * 128:(stream + 1) * 128],
                                start=True, stop=True)
                        t.drain()
                        s_pt.inc(t.nop(), 1)

                    if my_pt_index % 2 == 0:
                        @blk.vector
                        def _(v, ps=ps, g=g, klo=klo, khi=khi, nbank=nbank,
                              gpar=gpar, my=my_pt_index):
                            s_pt.wait(v)
                            if my >= 2:
                                v.wait_ge(s_pv.h, my - 1)
                            bank = g % nbank
                            cp = v.tensor_copy(
                                pts[:, gpar * 512: gpar * 512 + (khi - klo) * 128],
                                ps[:, bank * 512: bank * 512 + (khi - klo) * 128])
                            s_pe.inc(cp, 1)
                    else:
                        @blk.scalar
                        def _(sc, ps=ps, g=g, klo=klo, khi=khi, nbank=nbank,
                              gpar=gpar, my=my_pt_index):
                            s_pt.wait(sc)
                            if my >= 2:
                                sc.wait_ge(s_pv.h, my - 1)
                            bank = g % nbank
                            cp = sc.copy(
                                pts[:, gpar * 512: gpar * 512 + (khi - klo) * 128],
                                ps[:, bank * 512: bank * 512 + (khi - klo) * 128])
                            s_pe.inc(cp, 1)

                    @blk.tensor
                    def _(t, h=h, par=par, stream=stream, qt=qt, g=g, klo=klo,
                          khi=khi, nk=nk, gpar=gpar):
                        s_pe.wait(t)
                        if g == 0 and h >= 1:
                            t.wait_ge(s_cx.h, cx_hist[h - 1])
                        vh = vhb[:, par * TOK:(par + 1) * TOK]
                        for kk in range(klo, khi):
                            mm = t.matmul(
                                cxp[:, stream * 512 + (qt % 4) * 128:
                                    stream * 512 + (qt % 4 + 1) * 128],
                                vh[:, kk * 128:(kk + 1) * 128],
                                pts[:, gpar * 512 + (kk - klo) * 128:
                                    gpar * 512 + (kk - klo + 1) * 128],
                                start=(kk == 0), stop=(kk == nk - 1),
                                skip_group_check=True)
                        t.drain()
                        s_pv.inc(t.nop(), 1)
                    n_ptgroups += 1
                pe_after_qt[stream] = s_pe.n
                pt_after_qt[stream] = s_pt.n

        for stream in (0, 1):
            @blk.vector
            def _(v, h=h, stream=stream):
                s_pv.wait(v)
                cp = v.tensor_copy(CTX[:, h * OWN + stream * 512:
                                       h * OWN + (stream + 1) * 512],
                                   cxp[:, stream * 512:(stream + 1) * 512])
                s_cx.inc(cp, 1)
        cx_hist.append(s_cx.n)

    for gi in gamma_iter:
        emit_gamma_chunk(gi)

    @blk.vector
    def _(v):
        s_ga.wait(v)
        for wi in range(3):
            r = v.reduce_sum(gtot[:, 4 * wi:4 * wi + 1],
                             bass.AP(gpart, wi * 192, [[3 * 192, 128], [4, NWCH]]),
                             axis=AX)
            s_vch.inc(r, 1)
            s_vch.wait(v)
        s_ga.inc(v.nop(), 1)

    @blk.tensor
    def _(t):
        s_ga.wait(t)
        s_pe.wait(t)
        for wi in range(3):
            mm = t.matmul(sca[0:1, wi:wi + 1], gtot[:, 4 * wi:4 * wi + 1], ones[:, 0:1],
                          start=True, stop=True)
        t.drain()
        s_ga.inc(t.nop(), 1)

    @blk.scalar
    def _(sc):
        s_ga.wait(sc)
        cp = sc.activation(gth1[0:1, 0:3], sca[0:1, 0:3], AF.Copy,
                           scale=0.5 / float(NWEL), bias=0.5e-8)
        s_ach.inc(cp, 1)
        s_ach.wait(sc)
        s_ga.inc(sc.nop(), 1)

    @blk.tensor
    def _(t):
        s_ga.wait(t)
        for wi in range(3):
            mm = t.matmul(sca[:, 512 + wi:513 + wi], ones[0:1, :], gth1[0:1, wi:wi + 1],
                          start=True, stop=True)
        t.drain()
        s_ga.inc(t.nop(), 1)

    @blk.vector
    def _(v):
        s_ga.wait(v)
        cp = v.tensor_copy(thr_sb[:, 0:3], sca[:, 512:515])
        s_ga.inc(cp, 1)

    @blk.sync
    def _(s):
        if "thr" in tap_t:
            s_ga.wait(s)
            s_tap.inc(s.dma_start(tap_t["thr"][:, :], thr_sb[:, :]), 16)
        if "ctx" in tap_t:
            s_cx.wait(s)
            for h in range(NH):
                s_tap.inc(s.dma_start(tap_t["ctx"][h * 128:(h + 1) * 128, :],
                                      CTX[:, h * OWN:(h + 1) * OWN]), 16)

    ph2.close()
    open_stacks.remove(ph2)

    if stop_after == "ph2":
        return finish()

    ph3 = ExitStack()
    open_stacks.append(ph3)
    dstr = ph3.enter_context(nc.sbuf_tensor("dstr", [128, 2 * 8192], BF16))
    dso = ph3.enter_context(nc.sbuf_tensor("dso", [128, 4 * 512], BF16))
    d_ps = ph3.enter_context(nc.psum_tensor("d_ps", [128, 2048], F32))
    d3_hist, de_hist, sd_hist = [], [], []

    for si in range(4):
        @blk.gpsimd
        def _(g, si=si):
            if si >= 2:
                g.wait_ge(s_d3.h, d3_hist[si - 2])
            else:
                g.wait_ge(s_cx.h, 32)
                s_ga.wait(g)
            src = bass.AP(dense_w, si * 512, [[DM, 128], [128 * DM, 16], [1, 512]])
            s_w.inc(g.dma_start(dstr[:, (si % 2) * 8192:(si % 2 + 1) * 8192], src), 16)

        for oc4 in range(4):
            for tg in range(2):
                gl = si * 8 + oc4 * 2 + tg
                bank = gl % 4

                @blk.tensor
                def _(t, si=si, oc4=oc4, tg=tg, gl=gl, bank=bank):
                    if gl == 0:
                        s_cx.wait(t)
                        s_ga.wait(t)
                    t.wait_ge(s_w.h, s_w.n)
                    if gl >= 4:
                        t.wait_ge(s_de.h, de_hist[gl - 4])
                    wb = (si % 2) * 8192
                    for k in range(16):
                        mm = t.matmul(d_ps[:, bank * 512:(bank + 1) * 512],
                                      dstr[:, wb + k * 512 + oc4 * 128:
                                           wb + k * 512 + oc4 * 128 + 128],
                                      CTX[:, k * OWN + tg * 512: k * OWN + (tg + 1) * 512],
                                      start=(k == 0), stop=(k == 15))
                    t.drain()
                    s_d3.inc(t.nop(), 1)

                @blk.scalar
                def _(sc, gl=gl, bank=bank):
                    s_d3.wait(sc)
                    if gl >= 4:
                        sc.wait_ge(s_sd.h, sd_hist[gl - 4])
                    cp = sc.copy(dso[:, (gl % 4) * 512:(gl % 4 + 1) * 512],
                                 d_ps[:, bank * 512:(bank + 1) * 512])
                    s_de.inc(cp, 1)
                de_hist.append(s_de.n)

                @blk.sync
                def _(s, si=si, oc4=oc4, tg=tg, gl=gl):
                    s_de.wait(s)
                    oc = si * 4 + oc4
                    dma = s.dma_start(
                        bass.AP(s_d, (oc * 128) * OWN + tg * 512, [[OWN, 128], [1, 512]]),
                        dso[:, (gl % 4) * 512:(gl % 4 + 1) * 512])
                    s_sd.inc(dma, 16)
                sd_hist.append(s_sd.n)
        d3_hist.append(s_d3.n)

    @blk.sync
    def _(s):
        if "sd" in tap_t:
            s_sd.wait(s)
            s_tap.inc(s.dma_start(tap_t["sd"][:, :], s_d[:, :]), 16)

    ph3.close()
    open_stacks.remove(ph3)

    if stop_after == "ph3":
        return finish()

    phA.close()
    open_stacks.remove(phA)

    phB = ExitStack()
    open_stacks.append(phB)
    HIDB = phB.enter_context(nc.sbuf_tensor("HIDB", [128, NHC * OWN], BF16))
    ph4a = ExitStack()
    open_stacks.append(ph4a)
    xn_o = ph4a.enter_context(nc.sbuf_tensor("xn_o", [128, 16 * OWN], BF16))
    wfb = ph4a.enter_context(nc.sbuf_tensor("wfb", [128, 4 * DM], F32))
    qza = ph4a.enter_context(nc.sbuf_tensor("qza", [128, DM], BF16))
    qzb = ph4a.enter_context(nc.sbuf_tensor("qzb", [128, DM], BF16))
    qzw = ph4a.enter_context(nc.sbuf_tensor("qzw", [128, 2 * DM], BF16))
    wT = ph4a.enter_context(nc.sbuf_tensor("wT", [128, 4 * DM], BF16))
    sgh = ph4a.enter_context(nc.sbuf_tensor("sgh", [128, 512], BF16))
    tq_ps = ph4a.enter_context(nc.psum_tensor("tq_ps", [128, 4096], BF16))
    x_ps = ph4a.enter_context(nc.psum_tensor("x_ps", [128, 2048], F32))

    @blk.scalar
    def _(sc):
        sc.preload_act_table(AF.Sigmoid)

    @blk.gpsimd
    def _(s):
        s_xnd.wait(s)
        s_sd.wait(s)
        src = bass.AP(xn_d, 0, [[OWN, 128], [128 * OWN, 16], [1, OWN]])
        s_f.inc(s.dma_start(xn_o[:, :], src), 16)

    f_hist, qz_hist, wt_hist, we_hist, m4_hist, hv_hist = [], [], [], [], [], []

    for hc in range(NHC):
        par = hc % 2
        for wi, wsrc in ((0, w1), (1, w2)):
            li = 2 * hc + wi

            @blk.gpsimd
            def _(s, hc=hc, wi=wi, wsrc=wsrc, li=li, par=par):
                if li >= 4:
                    s.wait_ge(s_qz.h, qz_hist[li - 4])
                dma = s.dma_start(wfb[:, (2 * par + wi) * DM:(2 * par + wi + 1) * DM],
                                  wsrc[hc * 128:(hc + 1) * 128, :])
                s_f.inc(dma, 16)
            f_hist.append(s_f.n)

            @blk.vector
            def _(v, li=li, wi=wi, par=par):
                s_f.wait(v, f_hist[li])
                s_ga.wait(v)
                if li >= 2:
                    v.wait_ge(s_wt.h, wt_hist[li - 2])
                wf = wfb[:, (2 * par + wi) * DM:(2 * par + wi + 1) * DM]
                v.tensor_scalar(qza[:, :], wf, thr_sb[:, wi:wi + 1], None,
                                AluOpType.is_ge)
                v.tensor_scalar(qzb[:, :], wf, -1.0, thr_sb[:, wi:wi + 1],
                                AluOpType.mult, AluOpType.is_ge)
                q = v.tensor_tensor(qzw[:, wi * DM:(wi + 1) * DM], qza[:, :], qzb[:, :],
                                    AluOpType.subtract)
                s_qz.inc(q, 1)
            qz_hist.append(s_qz.n)

            @blk.tensor
            def _(t, li=li, wi=wi):
                s_qz.wait(t)
                if li < 2:
                    t.wait_ge(s_de.h, 32)
                if li >= 2:
                    t.wait_ge(s_we.h, we_hist[li - 2])
                for k in range(16):
                    mm = t.transpose(tq_ps[:, wi * 2048 + k * 128:wi * 2048 + (k + 1) * 128],
                                     qzw[:, wi * DM + k * 128: wi * DM + (k + 1) * 128],
                                     ident[:, :])
                t.drain()
                s_wt.inc(t.nop(), 1)
            wt_hist.append(s_wt.n)

            @blk.scalar
            def _(sc, li=li, wi=wi, par=par):
                s_wt.wait(sc)
                if li >= 4:
                    sc.wait_ge(s_m4.h, m4_hist[4 * (li // 2 - 2) + 2 * wi + 1])
                cp = sc.copy(wT[:, (2 * par + wi) * DM:(2 * par + wi + 1) * DM],
                             tq_ps[:, wi * 2048:(wi + 1) * 2048])
                s_we.inc(cp, 1)
            we_hist.append(s_we.n)

        for wi in range(2):
            for tg in range(2):
                bank = 2 * wi + tg

                @blk.tensor
                def _(t, hc=hc, wi=wi, tg=tg, par=par, bank=bank):
                    t.wait_ge(s_we.h, we_hist[2 * hc + wi])
                    if hc >= 1:
                        t.wait_ge(s_hv.h, hv_hist[2 * (hc - 1) + tg])
                    for k in range(16):
                        mm = t.matmul(x_ps[:, bank * 512:(bank + 1) * 512],
                                      wT[:, (2 * par + wi) * DM + k * 128:
                                         (2 * par + wi) * DM + (k + 1) * 128],
                                      xn_o[:, k * OWN + tg * 512: k * OWN + (tg + 1) * 512],
                                      start=(k == 0), stop=(k == 15))
                    t.drain()
                    s_m4.inc(t.nop(), 1)
                m4_hist.append(s_m4.n)

        for tg in range(2):
            @blk.scalar
            def _(sc, hc=hc, tg=tg):
                sc.wait_ge(s_m4.h, m4_hist[4 * hc + tg])
                s_hv.wait(sc)
                a = sc.activation(sgh[:, :], x_ps[:, tg * 512:(tg + 1) * 512],
                                  AF.Sigmoid)
                s_sg.inc(a, 1)

            @blk.vector
            def _(v, hc=hc, tg=tg):
                s_sg.wait(v)
                v.wait_ge(s_m4.h, m4_hist[4 * hc + 2 + tg])
                hh = v.tensor_tensor(HIDB[:, hc * OWN + tg * 512: hc * OWN + (tg + 1) * 512],
                                     sgh[:, :], x_ps[:, 1024 + tg * 512:1024 + (tg + 1) * 512],
                                     AluOpType.mult)
                s_hv.inc(hh, 1)
            hv_hist.append(s_hv.n)

    @blk.sync
    def _(s):
        if "hid" in tap_t:
            s_hv.wait(s)
            s_tap.inc(s.dma_start(tap_t["hid"][:, :], HIDB[:, :]), 16)

    ph4a.close()
    open_stacks.remove(ph4a)

    if stop_after == "ph4a":
        return finish()

    ph4b = ExitStack()
    open_stacks.append(ph4b)
    w3b = ph4b.enter_context(nc.sbuf_tensor("w3b", [128, 2 * HIDN], F32))
    qza3 = ph4b.enter_context(nc.sbuf_tensor("qza3", [128, HIDN], BF16))
    qzb3 = ph4b.enter_context(nc.sbuf_tensor("qzb3", [128, HIDN], BF16))
    qzw3 = ph4b.enter_context(nc.sbuf_tensor("qzw3", [128, HIDN], BF16))
    w3T = ph4b.enter_context(nc.sbuf_tensor("w3T", [128, 2 * HIDN], BF16))
    gch = ph4b.enter_context(nc.sbuf_tensor("gch", [128, 512], BF16))
    ssb = ph4b.enter_context(nc.sbuf_tensor("ssb", [128, 2 * OWN], BF16))
    sgo = ph4b.enter_context(nc.sbuf_tensor("sgo", [128, 2 * OWN], BF16))
    t3_ps = ph4b.enter_context(nc.psum_tensor("t3_ps", [128, 4096], BF16))
    f_ps = ph4b.enter_context(nc.psum_tensor("f_ps", [128, 1024], F32))

    b_qz, b_wt, b_we, b_m4, b_sg, b_fs, b_sgd = [], [], [], [], [], [], []

    @blk.scalar
    def _(sc):
        sc.preload_act_table(AF.Silu)

    for dc in range(16):
        par = dc % 2

        @blk.gpsimd
        def _(s, dc=dc, par=par):
            if dc == 0:
                s_m4.wait(s)
                s_hv.wait(s)
            if dc >= 2:
                s.wait_ge(s_qz.h, b_qz[dc - 2])
            s_f.inc(s.dma_start(w3b[:, par * HIDN:(par + 1) * HIDN],
                                w3[dc * 128:(dc + 1) * 128, :]), 16)
            if dc >= 2:
                s.wait_ge(s_sg.h, b_sg[dc - 2])
            s_fs.inc(s.dma_start(ssb[:, par * OWN:(par + 1) * OWN],
                                 bass.AP(s_d, dc * 128 * OWN, [[OWN, 128], [1, OWN]])), 16)
        b_fs.append(s_fs.n)

        @blk.vector
        def _(v, dc=dc):
            s_f.wait(v)
            s_ga.wait(v)
            if dc >= 1:
                v.wait_ge(s_wt.h, b_wt[dc - 1])
            wfv = w3b[:, (dc % 2) * HIDN:(dc % 2 + 1) * HIDN]
            v.tensor_scalar(qza3[:, :], wfv, thr_sb[:, 2:3], None, AluOpType.is_ge)
            v.tensor_scalar(qzb3[:, :], wfv, -1.0, thr_sb[:, 2:3],
                            AluOpType.mult, AluOpType.is_ge)
            q = v.tensor_tensor(qzw3[:, :], qza3[:, :], qzb3[:, :], AluOpType.subtract)
            s_qz.inc(q, 1)
        b_qz.append(s_qz.n)

        for rnd in range(3):
            lo, hi = 16 * rnd, min(16 * rnd + 16, NHC)
            ri = 3 * dc + rnd

            @blk.tensor
            def _(t, rnd=rnd, lo=lo, hi=hi, ri=ri):
                s_qz.wait(t)
                if ri < 2:
                    t.wait_ge(s_we.h, 2 * NHC)
                    s_hv.wait(t)
                if ri >= 2:
                    t.wait_ge(s_we.h, b_we[ri - 2])
                for k in range(lo, hi):
                    mm = t.transpose(
                        t3_ps[:, (rnd % 2) * 2048 + (k - lo) * 128:
                              (rnd % 2) * 2048 + (k - lo + 1) * 128],
                        qzw3[:, k * 128:(k + 1) * 128], ident[:, :])
                t.drain()
                s_wt.inc(t.nop(), 1)

            @blk.scalar
            def _(sc, dc=dc, rnd=rnd, lo=lo, hi=hi, par=par):
                s_wt.wait(sc)
                if dc >= 2:
                    sc.wait_ge(s_m4.h, b_m4[2 * (dc - 2) + 1])
                cp = sc.copy(w3T[:, par * HIDN + lo * 128: par * HIDN + hi * 128],
                             t3_ps[:, (rnd % 2) * 2048:(rnd % 2) * 2048 + (hi - lo) * 128])
                s_we.inc(cp, 1)
            b_we.append(s_we.n)
        b_wt.append(s_wt.n)

        for tg in range(2):
            @blk.tensor
            def _(t, dc=dc, tg=tg, par=par):
                t.wait_ge(s_we.h, b_we[3 * dc + 2])
                if dc == 0:
                    s_hv.wait(t)
                    t.wait_ge(s_sg.h, 2 * NHC)
                if dc >= 1:
                    t.wait_ge(s_sg.h, b_sg[dc - 1])
                for k in range(NHC):
                    mm = t.matmul(f_ps[:, tg * 512:(tg + 1) * 512],
                                  w3T[:, par * HIDN + k * 128: par * HIDN + (k + 1) * 128],
                                  HIDB[:, k * OWN + tg * 512: k * OWN + (tg + 1) * 512],
                                  start=(k == 0), stop=(k == NHC - 1))
                t.drain()
                s_m4.inc(t.nop(), 1)
            b_m4.append(s_m4.n)

            @blk.scalar
            def _(sc, tg=tg):
                s_m4.wait(sc)
                s_sg.wait(sc)
                a = sc.activation(gch[:, :], f_ps[:, tg * 512:(tg + 1) * 512], AF.Silu)
                s_sg.inc(a, 1)

            @blk.vector
            def _(v, dc=dc, tg=tg, par=par):
                s_sg.wait(v)
                s_fs.wait(v, b_fs[dc])
                if dc >= 2:
                    v.wait_ge(s_sgd.h, b_sgd[dc - 2])
                m = v.tensor_tensor(sgo[:, par * OWN + tg * 512: par * OWN + (tg + 1) * 512],
                                    gch[:, :],
                                    ssb[:, par * OWN + tg * 512: par * OWN + (tg + 1) * 512],
                                    AluOpType.mult)
                s_sg.inc(m, 1)
        b_sg.append(s_sg.n)

        @blk.sync
        def _(s, dc=dc, par=par):
            s_sg.wait(s)
            dma = s.dma_start(bass.AP(sg_d, dc * 128 * OWN, [[OWN, 128], [1, OWN]]),
                              sgo[:, par * OWN:(par + 1) * OWN])
            s_sgd.inc(dma, 16)
        b_sgd.append(s_sgd.n)

    @blk.sync
    def _(s):
        if "sg" in tap_t:
            s_sgd.wait(s)
            s_tap.inc(s.dma_start(tap_t["sg"][:, :], sg_d[:, :]), 16)

    ph4b.close()
    open_stacks.remove(ph4b)

    if stop_after == "ph4b":
        return finish()

    phf = ExitStack()
    open_stacks.append(phf)
    xo_sb = phf.enter_context(nc.sbuf_tensor("xo_sb", [128, 2 * DM], F32))
    sgf = phf.enter_context(nc.sbuf_tensor("sgf", [128, 2 * DM], BF16))
    ou_sb = phf.enter_context(nc.sbuf_tensor("ou_sb", [128, 2 * DM], F32))
    fT_ps = phf.enter_context(nc.psum_tensor("fT_ps", [128, 4096], BF16))

    fo_hist, out_hist = [], []
    for tb in range(8):
        par = tb % 2

        @blk.gpsimd
        def _(s, tb=tb, par=par):
            if tb == 0:
                s_sgd.wait(s)
            if tb >= 2:
                s.wait_ge(s_fo.h, fo_hist[tb - 2])
            s_fx.inc(s.dma_start(xo_sb[:, par * DM:(par + 1) * DM],
                                 x_own[tb * 128:(tb + 1) * 128, :]), 16)
            src = bass.AP(sg_d, tb * 128, [[OWN, 128], [128 * OWN, 16], [1, 128]])
            s_fx.inc(s.dma_start(sgf[:, par * DM:(par + 1) * DM], src), 16)

        @blk.tensor
        def _(t, tb=tb, par=par):
            s_fx.wait(t)
            if tb >= 2:
                t.wait_ge(s_fo.h, fo_hist[tb - 2])
            for dc2 in range(16):
                mm = t.transpose(fT_ps[:, par * 2048 + dc2 * 128: par * 2048 + (dc2 + 1) * 128],
                                 sgf[:, par * DM + dc2 * 128: par * DM + (dc2 + 1) * 128],
                                 ident[:, :])
            t.drain()
            s_ft.inc(t.nop(), 1)

        @blk.vector
        def _(v, tb=tb, par=par):
            s_ft.wait(v)
            if tb >= 2:
                v.wait_ge(s_out.h, out_hist[tb - 2])
            o = v.scalar_tensor_tensor(ou_sb[:, par * DM:(par + 1) * DM],
                                       fT_ps[:, par * 2048:(par + 1) * 2048], 1.0,
                                       xo_sb[:, par * DM:(par + 1) * DM],
                                       AluOpType.mult, AluOpType.add)
            s_fo.inc(o, 1)
        fo_hist.append(s_fo.n)

        @blk.sync
        def _(s, tb=tb, par=par):
            s_fo.wait(s)
            dma = s.dma_start(out_t[tb * 128:(tb + 1) * 128, :],
                              ou_sb[:, par * DM:(par + 1) * DM])
            s_out.inc(dma, 16)
        out_hist.append(s_out.n)

    phf.close()
    open_stacks.remove(phf)
    return finish()



_NC_CACHE = {}


def get_nc(taps=(), stop_after="full"):
    key = (tuple(sorted(taps)), stop_after)
    if key not in _NC_CACHE:
        _NC_CACHE[key] = build_nc(taps, stop_after)
    return _NC_CACHE[key]


def make_in_maps(inputs):
    x = np.ascontiguousarray(inputs["L0_attn_x"], dtype=np.float32)
    shared = {
        "qkv_w": np.ascontiguousarray(inputs["L0_attn_qkv_w"], np.float32),
        "dense_w": np.ascontiguousarray(inputs["L0_attn_dense_w"], np.float32),
        "w1": np.ascontiguousarray(inputs["L0_ffn_w1"], np.float32),
        "w2": np.ascontiguousarray(inputs["L0_ffn_w2"], np.float32),
        "w3": np.ascontiguousarray(inputs["L0_ffn_w3"], np.float32),
        "rel_t": np.ascontiguousarray(inputs["L0_attn_rel_attn_table"], np.float32),
    }
    consts = {0: _host_constants(0), 1: _host_constants(1)}
    maps = []
    for c in range(8):
        b, half = divmod(c, 2)
        xt = x[b].reshape(NT, 128, DM)
        perm = [(k + half if k % 2 == 0 else k - half) for k in range(NT)]
        m = dict(shared)
        m["x_loc"] = np.ascontiguousarray(xt[perm].reshape(TOK, DM))
        m["x_own"] = np.ascontiguousarray(
            xt[[2 * i + half for i in range(8)]].reshape(OWN, DM))
        cc = consts[half]
        for k in ("oh_e", "oh_o", "cq", "sq", "ck", "sk", "ident", "jmat",
                  "pswap", "ones_f"):
            m[k] = cc[k]
        maps.append(m)
    return maps


def kernel(**inputs):
    nc = get_nc()
    maps = make_in_maps(inputs)
    res = run_bass_kernel_spmd(nc, maps, core_ids=list(range(8)))
    out = np.empty((4, TOK, DM), np.float32)
    for c in range(8):
        b, half = divmod(c, 2)
        o = res.results[c]["out"].reshape(8, 128, DM)
        ot = out[b].reshape(NT, 128, DM)
        for i in range(8):
            ot[2 * i + half] = o[i]
    return out
